# version 01
# baseline (speedup 1.0000x reference)
import numpy as np
import concourse.bass as bass
import concourse.mybir as mybir
from concourse.bass_utils import run_bass_kernel_spmd

F32 = mybir.dt.float32
BF16 = mybir.dt.bfloat16
I32 = mybir.dt.int32
ALU = mybir.AluOpType
AXIS = mybir.AxisListType
ACT = mybir.ActivationFunctionType

D = 4096
H = 11008
NCORES = 8
T = 1024
NTT = T // 128
C = D // 128
EPS = 1e-5
MEGAS = [2560, 2560, 2560, 3328]
QCH = 1024

ENGS = ("sp", "ve", "ac", "pe", "po")


def ilv_index(DD):
    idx = np.empty(DD, dtype=np.int64)
    for g0 in range(0, DD, 512):
        gs = min(512, DD - g0)
        nb = gs // 16
        j = np.arange(gs)
        idx[g0:g0 + gs] = g0 + (j % nb) * 16 + (j // nb)
    return idx


class Plan:
    def __init__(self):
        self.steps = {e: [] for e in ENGS}
        self.counts = {e: 0 for e in ENGS}
        self.lastwait = {}

    def emit(self, eng, fn, reads=(), inc=1):
        waits = []
        for tok in reads:
            if tok is None:
                continue
            peng, pcnt = tok
            key = (eng, peng)
            if pcnt > self.lastwait.get(key, -1):
                self.lastwait[key] = pcnt
                waits.append((peng, pcnt))
        self.counts[eng] += inc
        tok = (eng, self.counts[eng])
        self.steps[eng].append((tuple(waits), fn, inc))
        return tok


class Slot:
    __slots__ = ("ready", "readers")

    def __init__(self):
        self.ready = None
        self.readers = []


def build_nc():
    nc = bass.Bass("TRN2", target_bir_lowering=False, debug=False)
    p = Plan()

    x_tok = nc.dram_tensor("x_tok", [T, D], F32, kind="ExternalInput").ap()
    x_tokU = nc.dram_tensor("x_tokU", [T, D], F32, kind="ExternalInput").ap()
    watt = nc.dram_tensor("watt", [D, D], F32, kind="ExternalInput").ap()
    mask_r = nc.dram_tensor("mask_r", [128, 128, 1024], I32, kind="ExternalInput").ap()
    bias_in = nc.dram_tensor("bias_in", [1, D], F32, kind="ExternalInput").ap()
    wg_h = nc.dram_tensor("wg_h", [H, D], F32, kind="ExternalInput").ap()
    wu_h = nc.dram_tensor("wu_h", [H, D], F32, kind="ExternalInput").ap()
    wd_h = nc.dram_tensor("wd_h", [D, H], F32, kind="ExternalInput").ap()
    scale_in = nc.dram_tensor("scale_in", [128, 32], F32, kind="ExternalInput").ap()
    lrA_in = nc.dram_tensor("lrA_in", [128, C * 64], F32, kind="ExternalInput").ap()
    lrB_in = nc.dram_tensor("lrB_in", [64, D], F32, kind="ExternalInput").ap()
    out_d = nc.dram_tensor("out", [T, D], F32, kind="ExternalOutput").ap()

    xnfm_d = nc.dram_tensor("xnfm_d", [C, 128, T], BF16, kind="Internal").ap()
    xqfm_d = nc.dram_tensor("xqfm_d", [C, 128, T], BF16, kind="Internal").ap()
    s_d = nc.dram_tensor("s_d", [T, D], BF16, kind="Internal").ap()
    acc_d = nc.dram_tensor("acc_d", [T, D], F32, kind="Internal").ap()
    bmT_d = nc.dram_tensor("bmT_d", [128, 128], F32, kind="Internal").ap()

    ctx = []

    def sb(name, shape, dt):
        t = nc.sbuf_tensor(name, shape, dt)
        ctx.append(t)
        return t.__enter__()

    def ps_(name, shape):
        t = nc.psum_tensor(name, shape, F32)
        ctx.append(t)
        return t.__enter__()

    def sem(name):
        t = nc.semaphore(name)
        ctx.append(t)
        return t.__enter__()

    arA = sb("arA", [128, D], F32)
    arB = sb("arB", [128, D], F32)
    arXN = sb("arXN", [128, D], F32)
    q_t = sb("q_t", [128, QCH], F32)
    q_a = sb("q_a", [128, QCH], F32)
    q_k1 = sb("q_k1", [128, QCH], I32)
    q_k2 = sb("q_k2", [128, QCH], I32)
    q_k3 = sb("q_k3", [128, QCH], I32)
    q_u2 = sb("q_u2", [128, QCH], BF16)
    q_sg = sb("q_sg", [128, QCH], BF16)
    q_m = sb("q_m", [128, D // 16], F32)
    q_rm = sb("q_rm", [128, D // 16], F32)
    q_m12 = sb("q_m12", [128, D // 16], BF16)
    wq_tok = sb("wq_tok", [128, D], BF16)
    slab = sb("slab", [128, 16384], BF16)
    wa_raw = [sb(f"wa_raw{i}", [128, 512], F32) for i in range(2)]
    wm = [sb(f"wm{i}", [128, 512], BF16) for i in range(2)]
    xs_st = [sb(f"xs_st{i}", [128, T], BF16) for i in range(2)]
    ev_st = [sb(f"ev_st{i}", [128, 512], BF16) for i in range(2)]
    KMAX = max(MEGAS) // 128
    hq_meg = sb("hq_meg", [128, KMAX, T], BF16)
    bmT_exp = hq_meg[:, 0:4, :].rearrange("p k (a b) -> p (k a) b", b=128)
    h_f = sb("h_f", [128, 512], F32)
    hq_tok = sb("hq_tok", [128, 512], BF16)
    dstage = [sb(f"dstage{i}", [128, 512], F32) for i in range(2)]
    lrA_sb = sb("lrA_sb", [128, C * 64], BF16)
    lrBB = sb("lrBB", [128, D + 256], BF16)
    lrB_sb = lrBB[0:64, 0:D]
    bias_bf = lrBB[64:65, 0:D]
    ones_bf = lrBB[64:65, D:D + 128]
    u_sb = sb("u_sb", [128, NTT * 128], BF16)
    uT = sb("uT", [128, NTT, 128], BF16)
    scale_sb = sb("scale_sb", [128, 32], F32)
    st = {k: sb("st_" + k, [128, 1], F32)
          for k in ("s1", "s2", "mean", "ex2", "m2", "var", "sd", "rstd")}
    bm_i = sb("bm_i", [128, 128], F32)
    bm_b = sb("bm_b", [128, 128], F32)

    q_u1 = q_k1
    mk_raw = slab[:, 0:8192].bitcast(I32).rearrange("p (a b) -> p a b", b=1024)
    trst = slab[:].rearrange("p (c b) -> p c b", b=128)
    wgqfm = slab[:].rearrange("p (c r b) -> p c r b", r=4, b=128)
    wdqfm = slab[:].rearrange("p (k r b) -> p k r b", r=4, b=128)[:, 0:KMAX, :, :]
    sz1 = arXN[:].rearrange("p (t f) -> p t f", f=512)

    psb = [ps_(f"psb{i}", [128, 512]) for i in range(8)]

    sems = {e: sem("sem_" + e) for e in ENGS}

    slots = {}

    def slot(obj, tag=""):
        key = (id(obj), tag)
        if key not in slots:
            slots[key] = Slot()
        return slots[key]

    def E(eng, fn, reads=(), writes=(), inc=1):
        toks = []
        for r in reads:
            if isinstance(r, Slot):
                if r.ready is not None:
                    toks.append(r.ready)
            elif r is not None:
                toks.append(r)
        for w in writes:
            toks.extend(w.readers)
            if w.ready is not None:
                toks.append(w.ready)
        tok = p.emit(eng, fn, toks, inc)
        for r in reads:
            if isinstance(r, Slot):
                r.readers.append(tok)
        for w in writes:
            w.ready = tok
            w.readers = []
        return tok

    sl_arA, sl_arB, sl_arXN = slot(arA), slot(arB), slot(arXN)
    sl_wq, sl_slab = slot(wq_tok), slot(slab)
    sl_ps = [slot(b) for b in psb]
    sl_wa = [slot(w) for w in wa_raw]
    sl_wm = [slot(w) for w in wm]
    sl_xs = [slot(x) for x in xs_st]
    sl_ev = [slot(x) for x in ev_st]
    sl_dst = [slot(d) for d in dstage]
    sl_hf, sl_hqt = slot(h_f), slot(hq_tok)
    sl_usb, sl_uT = slot(u_sb), slot(uT)

    q_slots = {name: Slot() for name in
               ("t", "a", "k1", "k2", "k3", "u1", "u2", "sg", "m", "rm", "m12")}

    def emit_qdq(src_ap, dst_ap, F, tail256, scale_col, pass_views,
                 src_slot, dst_slot, extra_reads=()):
        nfull = (F - 256) // 512 if tail256 else F // 512
        nb_total = F // 16
        reads0 = [src_slot] + list(extra_reads)
        sm, srm, sm12 = q_slots["m"], q_slots["rm"], q_slots["m12"]
        if nfull > 0:
            E("ve", lambda v, nf=nfull: v.tensor_reduce(
                out=q_m[:, 0:nf * 32],
                in_=src_ap(0, nf * 512).rearrange("p (g k b) -> p g k b", k=16, b=32)
                    .transpose([0, 1, 3, 2]),
                axis=AXIS.X, op=ALU.max, apply_absolute_value=True),
              reads=reads0, writes=[sm])
        if tail256:
            E("ve", lambda v, g0=nfull * 512, nf=nfull: v.tensor_reduce(
                out=q_m[:, nf * 32:nf * 32 + 16],
                in_=src_ap(g0, 256).rearrange("p (k b) -> p k b", k=16, b=16)
                    .transpose([0, 2, 1]),
                axis=AXIS.X, op=ALU.max, apply_absolute_value=True),
              reads=reads0, writes=[sm])
        E("ve", lambda v, n=nb_total: v.tensor_scalar_max(q_rm[:, 0:n], q_m[:, 0:n], 1e-30),
          reads=[sm], writes=[srm])
        E("ve", lambda v, n=nb_total: v.reciprocal(out=q_rm[:, 0:n], in_=q_rm[:, 0:n]),
          reads=[srm], writes=[srm])
        if scale_col is None:
            E("ve", lambda v, n=nb_total: v.tensor_scalar_mul(
                q_m12[:, 0:n], q_m[:, 0:n], 1.0 / 12.0),
              reads=[sm], writes=[sm12])
        else:
            E("ve", lambda v, n=nb_total, sc=scale_col: v.tensor_scalar(
                out=q_m12[:, 0:n], in0=q_m[:, 0:n],
                scalar1=1.0 / 12.0, scalar2=sc, op0=ALU.mult, op1=ALU.mult),
              reads=[sm] + list(extra_reads), writes=[sm12])

        off = 0
        while off < F:
            ch = min(QCH, F - off)
            pieces = []
            o = off
            while o < off + ch:
                if tail256 and o >= nfull * 512:
                    pieces.append((o, 256))
                    o += 256
                else:
                    w = min(off + ch, nfull * 512 if tail256 else F) - o
                    pieces.append((o, w))
                    o += w
            st_, sa, sk1, sk2, sk3, su1, su2, ssg = (
                q_slots["t"], q_slots["a"], q_slots["k1"], q_slots["k2"],
                q_slots["k3"], q_slots["u1"], q_slots["u2"], q_slots["sg"])
            for (o, gs) in pieces:
                if gs >= 512:
                    ng = gs // 512
                    vi = src_ap(o, gs).rearrange("p (g k b) -> p g k b", k=16, b=32)
                    vr = (q_rm[:, o // 16:(o + gs) // 16]
                          .rearrange("p (g b) -> p g b", b=32)
                          .unsqueeze(2).broadcast_to([128, ng, 16, 32]))
                    vt = (q_t[:, (o - off):(o - off) + gs]
                          .rearrange("p (g k b) -> p g k b", k=16, b=32))
                else:
                    vi = src_ap(o, gs).rearrange("p (k b) -> p k b", k=16, b=16)
                    vr = (q_rm[:, o // 16:o // 16 + 16]
                          .unsqueeze(1).broadcast_to([128, 16, 16]))
                    vt = (q_t[:, (o - off):(o - off) + gs]
                          .rearrange("p (k b) -> p k b", k=16, b=16))
                E("ve", lambda v, a=vi, b=vr, c=vt: v.tensor_tensor(
                    out=c, in0=a, in1=b, op=ALU.mult),
                  reads=reads0 + [srm], writes=[st_])
            E("ac", lambda s, c=ch: s.activation(out=q_a[:, 0:c], in_=q_t[:, 0:c],
                                                 func=ACT.Abs),
              reads=[st_], writes=[sa])
            E("ac", lambda s, c=ch: s.sign(out=q_sg[:, 0:c], in_=q_t[:, 0:c]),
              reads=[st_], writes=[ssg])
            E("ve", lambda v, c=ch: v.tensor_scalar(
                out=q_k1[:, 0:c], in0=q_a[:, 0:c], scalar1=12.0, scalar2=0.0,
                op0=ALU.mult, op1=ALU.add),
              reads=[sa], writes=[sk1])
            E("ve", lambda v, c=ch: v.tensor_scalar(
                out=q_k2[:, 0:c], in0=q_a[:, 0:c], scalar1=6.0, scalar2=-2.0,
                op0=ALU.mult, op1=ALU.add),
              reads=[sa], writes=[sk2])
            E("ve", lambda v, c=ch: v.tensor_scalar(
                out=q_k3[:, 0:c], in0=q_a[:, 0:c], scalar1=6.0, scalar2=-4.5,
                op0=ALU.mult, op1=ALU.add),
              reads=[sa], writes=[sk3])
            E("ve", lambda v, c=ch: v.tensor_scalar(
                out=q_k1[:, 0:c], in0=q_k1[:, 0:c], scalar1=4, scalar2=0,
                op0=ALU.min, op1=ALU.max),
              reads=[sk1], writes=[sk1])
            E("ve", lambda v, c=ch: v.tensor_scalar(
                out=q_k2[:, 0:c], in0=q_k2[:, 0:c], scalar1=2, scalar2=0,
                op0=ALU.min, op1=ALU.max),
              reads=[sk2], writes=[sk2])
            E("ve", lambda v, c=ch: v.tensor_scalar(
                out=q_k3[:, 0:c], in0=q_k3[:, 0:c], scalar1=1, scalar2=0,
                op0=ALU.min, op1=ALU.max),
              reads=[sk3], writes=[sk3])
            E("ve", lambda v, c=ch: v.scalar_tensor_tensor(
                out=q_u1[:, 0:c], in0=q_k2[:, 0:c], scalar=2, in1=q_k1[:, 0:c],
                op0=ALU.mult, op1=ALU.add),
              reads=[sk2], writes=[sk1])
            E("ve", lambda v, c=ch: v.scalar_tensor_tensor(
                out=q_k2[:, 0:c], in0=q_k3[:, 0:c], scalar=4, in1=q_u1[:, 0:c],
                op0=ALU.mult, op1=ALU.add),
              reads=[sk3, sk1], writes=[sk2])
            E("ve", lambda v, c=ch: v.tensor_tensor(
                out=q_u2[:, 0:c], in0=q_k2[:, 0:c], in1=q_sg[:, 0:c], op=ALU.mult),
              reads=[sk2, ssg], writes=[su2])
            for (o, gs) in pieces:
                if gs >= 512:
                    ng = gs // 512
                    vu = (q_u2[:, (o - off):(o - off) + gs]
                          .rearrange("p (g k b) -> p g k b", k=16, b=32))
                    vm = (q_m12[:, o // 16:(o + gs) // 16]
                          .rearrange("p (g b) -> p g b", b=32)
                          .unsqueeze(2).broadcast_to([128, ng, 16, 32]))
                    vd = dst_ap(o, gs).rearrange("p (g k b) -> p g k b", k=16, b=32)
                else:
                    vu = (q_u2[:, (o - off):(o - off) + gs]
                          .rearrange("p (k b) -> p k b", k=16, b=16))
                    vm = (q_m12[:, o // 16:o // 16 + 16]
                          .unsqueeze(1).broadcast_to([128, 16, 16]))
                    vd = dst_ap(o, gs).rearrange("p (k b) -> p k b", k=16, b=16)
                E("ve", lambda v, a=vu, b=vm, c=vd: v.tensor_tensor(
                    out=c, in0=a, in1=b, op=ALU.mult),
                  reads=[su2, sm12], writes=[dst_slot])
            off += ch
        if pass_views is not None:
            sv, dv = pass_views
            E("ve", lambda v, a=sv, b=dv: v.tensor_copy(out=b, in_=a),
              reads=reads0, writes=[dst_slot])

    def upass_views(src, dst):
        sv = src[:, 3584:4096].rearrange("p (k b) -> p k b", b=32)[:, :, 24:32]
        dv = dst[:, 3584:4096].rearrange("p (k b) -> p k b", b=32)[:, :, 24:32]
        return sv, dv

    sl_lrA, sl_lrB = slot(lrA_sb), slot(lrB_sb)
    sl_bias, sl_scale, sl_ones = slot(bias_bf), slot(scale_sb), slot(ones_bf)
    E("po", lambda g: g.dma_start(out=lrA_sb[:], in_=lrA_in[:, :]), writes=[sl_lrA], inc=16)
    E("po", lambda g: g.dma_start(out=lrB_sb, in_=lrB_in[:, :]), writes=[sl_lrB], inc=16)
    E("po", lambda g: g.dma_start(out=bias_bf, in_=bias_in[:, :]), writes=[sl_bias], inc=16)
    E("po", lambda g: g.dma_start(out=scale_sb[:], in_=scale_in[:, :]), writes=[sl_scale], inc=16)
    E("ve", lambda v: v.memset(ones_bf, 1.0), writes=[sl_ones])

    sl_mk, sl_bmi, sl_bmb = slot(slab, "mk"), slot(bm_i), slot(bm_b)
    for kc in range(32):
        E("sp", lambda s, kc=kc: s.dma_start(
            out=mk_raw[:], in_=mask_r[:, kc * 4:(kc + 1) * 4, :]),
          writes=[sl_slab], inc=16)
        E("ve", lambda v, kc=kc: v.tensor_reduce(
            out=bm_i[:, kc * 4:(kc + 1) * 4], in_=mk_raw[:],
            axis=AXIS.X, op=ALU.add),
          reads=[sl_slab], writes=[sl_bmi])
    E("ve", lambda v: v.tensor_scalar(out=bm_b[:], in0=bm_i[:], scalar1=0, scalar2=None,
                                      op0=ALU.is_gt),
      reads=[sl_bmi], writes=[sl_bmb])
    tok_bmd = None
    for i4 in range(4):
        tok_bmd = E("po", lambda g, i4=i4: g.dma_start(
            out=bmT_d[:, 32 * i4:32 * (i4 + 1)].transpose([1, 0]),
            in_=bm_b[32 * i4:32 * (i4 + 1), :],
            allow_slow_non_contiguous=True),
          reads=[sl_bmb], inc=16)
    sl_bmE = sl_hq = slot(hq_meg, "bmE")
    for cc in range(C):
        for i in range(4):
            E("po", lambda g, cc=cc, i=i: g.dma_start(
                out=bmT_exp[32 * i:32 * (i + 1), cc, :],
                in_=bmT_d[4 * cc + i:4 * cc + i + 1, :].broadcast_to([32, 128])),
              reads=[tok_bmd], writes=[sl_bmE], inc=16)

    uview = u_sb[:].rearrange("p (tt f) -> p tt f", f=128)
    E("ve", lambda v: v.memset(u_sb[:], 0.0), writes=[sl_usb])

    for tt in range(NTT):
        r0 = tt * 128
        E("sp", lambda s, r0=r0: s.dma_start(out=arA[:], in_=x_tok[r0:r0 + 128, :]),
          writes=[sl_arA], inc=16)
        E("ve", lambda v: v.reduce_sum(out=st["s1"][:], in_=arA[:], axis=AXIS.X),
          reads=[sl_arA], writes=[slot(st["s1"])])
        E("ac", lambda s: s.activation(out=arXN[:], in_=arA[:], func=ACT.Square,
                                       accum_out=st["s2"][:]),
          reads=[sl_arA], writes=[sl_arXN, slot(st["s2"])])
        E("ve", lambda v: v.tensor_scalar_mul(st["mean"][:], st["s1"][:], 1.0 / D),
          reads=[slot(st["s1"])], writes=[slot(st["mean"])])
        E("ve", lambda v: v.tensor_scalar_mul(st["ex2"][:], st["s2"][:], 1.0 / D),
          reads=[slot(st["s2"])], writes=[slot(st["ex2"])])
        E("ve", lambda v: v.tensor_tensor(out=st["m2"][:], in0=st["mean"][:],
                                          in1=st["mean"][:], op=ALU.mult),
          reads=[slot(st["mean"])], writes=[slot(st["m2"])])
        E("ve", lambda v: v.tensor_tensor(out=st["var"][:], in0=st["ex2"][:],
                                          in1=st["m2"][:], op=ALU.subtract),
          reads=[slot(st["ex2"]), slot(st["m2"])], writes=[slot(st["var"])])
        E("ve", lambda v: v.tensor_scalar_add(st["var"][:], st["var"][:], EPS),
          reads=[slot(st["var"])], writes=[slot(st["var"])])
        E("ac", lambda s: s.activation(out=st["sd"][:], in_=st["var"][:], func=ACT.Sqrt),
          reads=[slot(st["var"])], writes=[slot(st["sd"])])
        E("ve", lambda v: v.reciprocal(out=st["rstd"][:], in_=st["sd"][:]),
          reads=[slot(st["sd"])], writes=[slot(st["rstd"])])
        E("ve", lambda v: v.tensor_scalar(out=arXN[:], in0=arA[:], scalar1=st["mean"][:],
                                          scalar2=st["rstd"][:], op0=ALU.subtract,
                                          op1=ALU.mult),
          reads=[sl_arA, slot(st["mean"]), slot(st["rstd"])], writes=[sl_arXN])
        E("ve", lambda v: v.tensor_copy(out=wq_tok[:], in_=arXN[:]),
          reads=[sl_arXN], writes=[sl_wq])
        E("sp", lambda s: s.dma_start(out=trst[:, 0:C, :], in_=wq_tok[:], transpose=True),
          reads=[sl_wq], writes=[sl_slab], inc=16)
        E("po", lambda g, r0=r0: g.dma_start(
            out=xnfm_d[:, :, r0:r0 + 128].transpose([1, 0, 2]), in_=trst[:, 0:C, :]),
          reads=[sl_slab], inc=16)
        lrA_v = lrA_sb[:].rearrange("p (c f) -> p c f", f=64)
        for cc in range(C):
            E("pe", lambda t, cc=cc: t.matmul(
                psb[0][0:128, 0:64], trst[:, cc, :], lrA_v[:, cc, :],
                start=(cc == 0), stop=(cc == C - 1)),
              reads=[sl_slab, sl_lrA],
              writes=([sl_ps[0]] if cc in (0, C - 1) else []))
        E("ve", lambda v, tt=tt: v.tensor_copy(out=uview[:, tt, 0:64],
                                               in_=psb[0][0:128, 0:64]),
          reads=[sl_ps[0]], writes=[sl_usb])
        E("sp", lambda s, r0=r0: s.dma_start(out=arB[:], in_=x_tokU[r0:r0 + 128, :]),
          writes=[sl_arB], inc=16)
        E("ve", lambda v: v.tensor_scalar(out=arB[:], in0=arB[:], scalar1=st["mean"][:],
                                          scalar2=st["rstd"][:], op0=ALU.subtract,
                                          op1=ALU.mult),
          reads=[sl_arB, slot(st["mean"]), slot(st["rstd"])], writes=[sl_arB])
        emit_qdq(lambda o, n: arB[:, o:o + n], lambda o, n: wq_tok[:, o:o + n],
                 D, False, None, upass_views(arB, wq_tok), sl_arB, sl_wq)
        E("sp", lambda s: s.dma_start(out=trst[:, 0:C, :], in_=wq_tok[:], transpose=True),
          reads=[sl_wq], writes=[sl_slab], inc=16)
        E("po", lambda g, r0=r0: g.dma_start(
            out=xqfm_d[:, :, r0:r0 + 128].transpose([1, 0, 2]), in_=trst[:, 0:C, :]),
          reads=[sl_slab], inc=16)

    E("sp", lambda s: s.dma_start(out=uT[:], in_=u_sb[:], transpose=True),
      reads=[sl_usb], writes=[sl_uT], inc=16)

    for nf in range(8):
        for cc in range(C):
            ib = cc % 2
            E("sp", lambda s, nf=nf, cc=cc, ib=ib: s.dma_start(
                out=wa_raw[ib][:],
                in_=watt[cc * 128:(cc + 1) * 128, nf * 512:(nf + 1) * 512]),
              writes=[sl_wa[ib]], inc=16)
            E("ve", lambda v, nf=nf, ib=ib, cc=cc: v.tensor_tensor(
                out=wm[ib][:].rearrange("p (nb w) -> p nb w", w=32),
                in0=wa_raw[ib][:].rearrange("p (nb w) -> p nb w", w=32),
                in1=bmT_exp[:, cc, nf * 16:(nf + 1) * 16].unsqueeze(2)
                    .broadcast_to([128, 16, 32]),
                op=ALU.mult),
              reads=[sl_wa[ib], sl_bmE], writes=[sl_wm[ib]])
            E("ac", lambda s, cc=cc, ib=ib: s.dma_start(
                out=xs_st[ib][:], in_=xnfm_d[cc, :, :]),
              writes=[sl_xs[ib]], inc=16)
            for tt in range(NTT):
                E("pe", lambda t, cc=cc, tt=tt, ib=ib: t.matmul(
                    psb[tt][:, :], xs_st[ib][:, tt * 128:(tt + 1) * 128], wm[ib][:],
                    start=(cc == 0), stop=False),
                  reads=[sl_xs[ib], sl_wm[ib]],
                  writes=([sl_ps[tt]] if cc == 0 else []))
        for tt in range(NTT):
            E("pe", lambda t, nf=nf, tt=tt: t.matmul(
                psb[tt][:, :], ones_bf, bias_bf[:, nf * 512:(nf + 1) * 512],
                start=False, stop=True),
              reads=[sl_ones, sl_bias], writes=[sl_ps[tt]])
        for tt in range(NTT):
            ib = tt % 2
            E("ve", lambda v, tt=tt, ib=ib: v.tensor_copy(out=ev_st[ib][:],
                                                          in_=psb[tt][:, :]),
              reads=[sl_ps[tt]], writes=[sl_ev[ib]])
            E("po", lambda g, nf=nf, tt=tt, ib=ib: g.dma_start(
                out=s_d[tt * 128:(tt + 1) * 128, nf * 512:(nf + 1) * 512],
                in_=ev_st[ib][:]),
              reads=[sl_ev[ib]], inc=16)

    wgq_view = slab[:].rearrange("p (c r b) -> p c r b", r=4, b=128)
    mega_off = 0
    for mi, M in enumerate(MEGAS):
        KM = M // 128
        hf_tiles = []
        o = 0
        while o < M:
            ts = min(512, M - o)
            hf_tiles.append((o, ts))
            o += ts
        for (ho, hts) in hf_tiles:
            g256 = (mi == len(MEGAS) - 1 and mega_off + ho + hts > H - 256)
            for zi, wh in ((0, wg_h), (1, wu_h)):
                for rb in range(hts // 128):
                    row0 = mega_off + ho + rb * 128
                    src, ssl = (arA, sl_arA) if rb % 2 == 0 else (arB, sl_arB)
                    E("sp", lambda s, wh=wh, row0=row0, src=src: s.dma_start(
                        out=src[:], in_=wh[row0:row0 + 128, :]),
                      writes=[ssl], inc=16)
                    emit_qdq(lambda o_, n_, src=src: src[:, o_:o_ + n_],
                             lambda o_, n_: wq_tok[:, o_:o_ + n_],
                             D, False, None, upass_views(src, wq_tok), ssl, sl_wq)
                    E("sp", lambda s, rb=rb: s.dma_start(
                        out=wgq_view[:, :, rb, :], in_=wq_tok[:], transpose=True),
                      reads=[sl_wq], writes=[sl_slab], inc=16)
                for cc in range(C):
                    ib = cc % 2
                    E("ac", lambda s, cc=cc, ib=ib: s.dma_start(
                        out=xs_st[ib][:], in_=xqfm_d[cc, :, :]),
                      writes=[sl_xs[ib]], inc=16)
                    for tt in range(NTT):
                        E("pe", lambda t, cc=cc, ib=ib, tt=tt, hts=hts: t.matmul(
                            psb[tt][:, 0:hts],
                            xs_st[ib][:, tt * 128:(tt + 1) * 128],
                            slab[:].rearrange("p (c f) -> p c f", f=512)[:, cc, 0:hts],
                            start=(cc == 0), stop=(cc == C - 1)),
                          reads=[sl_xs[ib], sl_slab],
                          writes=([sl_ps[tt]] if cc in (0, C - 1) else []))
                if zi == 0:
                    for tt in range(NTT):
                        E("ac", lambda s, tt=tt, hts=hts: s.activation(
                            out=sz1[:, tt, 0:hts], in_=psb[tt][:, 0:hts], func=ACT.Silu),
                          reads=[sl_ps[tt]], writes=[sl_arXN])
            for tt in range(NTT):
                E("ve", lambda v, tt=tt, hts=hts: v.tensor_tensor(
                    out=h_f[:, 0:hts], in0=sz1[:, tt, 0:hts], in1=psb[tt][:, 0:hts],
                    op=ALU.mult),
                  reads=[sl_arXN, sl_ps[tt]], writes=[sl_hf])
                pv = None
                if g256:
                    base = 10752 - (mega_off + ho)
                    sv = h_f[:, base:base + 256].rearrange(
                        "p (k b) -> p k b", b=16)[:, :, 8:16]
                    dv = hq_tok[:, base:base + 256].rearrange(
                        "p (k b) -> p k b", b=16)[:, :, 8:16]
                    pv = (sv, dv)
                emit_qdq(lambda o_, n_: h_f[:, o_:o_ + n_],
                         lambda o_, n_: hq_tok[:, o_:o_ + n_],
                         hts, g256, None, pv, sl_hf, sl_hqt)
                E("sp", lambda s, ho=ho, tt=tt, hts=hts: s.dma_start(
                    out=hq_meg[:, ho // 128:(ho + hts) // 128, tt * 128:(tt + 1) * 128],
                    in_=hq_tok[:, 0:hts], transpose=True),
                  reads=[sl_hqt], writes=[sl_hq], inc=16)
        g256d = (mi == len(MEGAS) - 1)
        for df in range(8):
            for dfb in range(4):
                row0 = df * 512 + dfb * 128
                src, ssl = (arA, sl_arA) if dfb % 2 == 0 else (arB, sl_arB)
                E("sp", lambda s, row0=row0, src=src, M=M, mo=mega_off: s.dma_start(
                    out=src[:, 0:M], in_=wd_h[row0:row0 + 128, mo:mo + M]),
                  writes=[ssl], inc=16)
                pv = None
                if g256d:
                    base = 10752 - mega_off
                    sv = src[:, base:base + 256].rearrange(
                        "p (k b) -> p k b", b=16)[:, :, 8:16]
                    dv = wq_tok[:, base:base + 256].rearrange(
                        "p (k b) -> p k b", b=16)[:, :, 8:16]
                    pv = (sv, dv)
                scol = scale_sb[:, df * 4 + dfb:df * 4 + dfb + 1]
                emit_qdq(lambda o_, n_, src=src: src[:, o_:o_ + n_],
                         lambda o_, n_: wq_tok[:, o_:o_ + n_],
                         M, g256d, scol, pv, ssl, sl_wq, extra_reads=[sl_scale])
                E("sp", lambda s, dfb=dfb, KM=KM, M=M: s.dma_start(
                    out=wdqfm[:, 0:KM, dfb, :], in_=wq_tok[:, 0:M], transpose=True),
                  reads=[sl_wq], writes=[sl_slab], inc=16)
            for tt in range(NTT):
                for k in range(KM):
                    last = (k == KM - 1 and mi != 0)
                    E("pe", lambda t, k=k, tt=tt: t.matmul(
                        psb[tt][:, :], hq_meg[:, k, tt * 128:(tt + 1) * 128],
                        wdqfm[:, k, :, :].rearrange("p r b -> p (r b)"),
                        start=(k == 0), stop=(k == KM - 1 and mi != 0)),
                      reads=[sl_hq, sl_slab],
                      writes=([sl_ps[tt]] if (k == 0 or last) else []))
                if mi == 0:
                    E("pe", lambda t, tt=tt, df=df: t.matmul(
                        psb[tt][:, :], uT[0:64, tt, :],
                        lrB_sb[:, df * 512:(df + 1) * 512],
                        start=False, stop=True),
                      reads=[sl_uT, sl_lrB], writes=[sl_ps[tt]])
                ib = tt % 2
                E("ve", lambda v, tt=tt, ib=ib: v.tensor_copy(
                    out=dstage[ib][:], in_=psb[tt][:, :]),
                  reads=[sl_ps[tt]], writes=[sl_dst[ib]])
                E("po", lambda g, tt=tt, df=df, ib=ib, mi=mi: g.dma_start(
                    out=acc_d[tt * 128:(tt + 1) * 128, df * 512:(df + 1) * 512],
                    in_=dstage[ib][:],
                    accum_op=(ALU.bypass if mi == 0 else ALU.add)),
                  reads=[sl_dst[ib]], inc=16)
        mega_off += M

    for tt in range(NTT):
        r0 = tt * 128
        E("sp", lambda s, r0=r0: s.dma_start(out=arA[:], in_=x_tok[r0:r0 + 128, :]),
          writes=[sl_arA], inc=16)
        E("po", lambda g, r0=r0: g.dma_start(out=arB[:], in_=acc_d[r0:r0 + 128, :]),
          writes=[sl_arB], inc=16)
        E("sp", lambda s, r0=r0: s.dma_start(out=wq_tok[:], in_=s_d[r0:r0 + 128, :]),
          writes=[sl_wq], inc=16)
        E("ac", lambda s: s.activation(out=arXN[:], in_=arB[:], func=ACT.Silu),
          reads=[sl_arB], writes=[sl_arXN])
        E("ve", lambda v: v.tensor_tensor(out=arXN[:], in0=arXN[:], in1=wq_tok[:],
                                          op=ALU.mult),
          reads=[sl_arXN, sl_wq], writes=[sl_arXN])
        E("ve", lambda v: v.tensor_tensor(out=arB[:], in0=arXN[:], in1=arA[:],
                                          op=ALU.add),
          reads=[sl_arXN, sl_arA], writes=[sl_arB])
        E("po", lambda g, r0=r0: g.dma_start(out=out_d[r0:r0 + 128, :], in_=arB[:]),
          reads=[sl_arB], inc=16)

    with nc.Block() as block:
        def runner(engname, handle):
            for (waits, fn, inc) in p.steps[engname]:
                for (peng, pcnt) in waits:
                    handle.wait_ge(sems[peng], pcnt)
                fn(handle).then_inc(sems[engname], inc)
            if engname == "po":
                handle.wait_ge(sems["po"], p.counts["po"])

        @block.sync
        def _(h):
            runner("sp", h)

        @block.vector
        def _(h):
            runner("ve", h)

        @block.scalar
        def _(h):
            runner("ac", h)

        @block.tensor
        def _(h):
            runner("pe", h)

        @block.gpsimd
        def _(h):
            runner("po", h)

    for t in reversed(ctx):
        t.__exit__(None, None, None)
    return nc


_NC_CACHE = {}


def kernel(L0_attn_data, L0_attn_mask, L0_attn_weight, L0_attn_bias,
           L0_ffn_w_gate, L0_ffn_w_up, L0_ffn_w_down, L0_ffn_mlp_output_scale,
           L0_ffn_lr_A, L0_ffn_lr_B, L0_ffn_up_reorder_index,
           L0_ffn_down_reorder_index):
    x = np.ascontiguousarray(np.asarray(L0_attn_data, dtype=np.float32).reshape(8192, D))
    up_perm = np.asarray(L0_ffn_up_reorder_index, dtype=np.int64)
    down_perm = np.asarray(L0_ffn_down_reorder_index, dtype=np.int64)
    pU = up_perm[ilv_index(D)]
    hsel = down_perm[ilv_index(H)]

    watt_T = np.ascontiguousarray(np.asarray(L0_attn_weight, dtype=np.float32).T)
    mask_r = np.ascontiguousarray(
        np.asarray(L0_attn_mask, dtype=np.int32)
        .reshape(128, 32, 128, 32).transpose(0, 2, 1, 3).reshape(128, 128, 1024))
    bias_in = np.ascontiguousarray(np.asarray(L0_attn_bias, dtype=np.float32)[None, :])
    wg_hh = np.ascontiguousarray(np.asarray(L0_ffn_w_gate, dtype=np.float32)[hsel][:, pU])
    wu_hh = np.ascontiguousarray(np.asarray(L0_ffn_w_up, dtype=np.float32)[hsel][:, pU])
    wd_hh = np.ascontiguousarray(np.asarray(L0_ffn_w_down, dtype=np.float32)[:, hsel])
    scale_in = np.ascontiguousarray(
        np.asarray(L0_ffn_mlp_output_scale, dtype=np.float32).reshape(32, 128).T)
    lrA_in = np.ascontiguousarray(
        np.asarray(L0_ffn_lr_A, dtype=np.float32).reshape(C, 128, 64)
        .transpose(1, 0, 2).reshape(128, C * 64))
    lrB_in = np.ascontiguousarray(np.asarray(L0_ffn_lr_B, dtype=np.float32))

    if "main" not in _NC_CACHE:
        _NC_CACHE["main"] = build_nc()
    nc = _NC_CACHE["main"]

    in_maps = []
    for ci in range(NCORES):
        xs = np.ascontiguousarray(x[ci * T:(ci + 1) * T, :])
        xsU = np.ascontiguousarray(xs[:, pU])
        in_maps.append({
            "x_tok": xs, "x_tokU": xsU, "watt": watt_T, "mask_r": mask_r,
            "bias_in": bias_in, "wg_h": wg_hh, "wu_h": wu_hh, "wd_h": wd_hh,
            "scale_in": scale_in, "lrA_in": lrA_in, "lrB_in": lrB_in,
        })
    res = run_bass_kernel_spmd(nc, in_maps, core_ids=list(range(NCORES)))
    out = np.concatenate([np.asarray(r["out"]) for r in res.results], axis=0)
    return out.reshape(2, 4096, D)


# version 04 (final)
# speedup vs baseline: 1.6721x; 1.6721x over previous
import numpy as np
import concourse.bass as bass
import concourse.mybir as mybir
from concourse.bass_utils import run_bass_kernel_spmd

F32 = mybir.dt.float32
BF16 = mybir.dt.bfloat16
I32 = mybir.dt.int32
ALU = mybir.AluOpType
AXIS = mybir.AxisListType
ACT = mybir.ActivationFunctionType

D = 4096
H = 11008
HP = 11264
SH_WG = HP // 8
SH_WD = D // 8
NCORES = 8
T = 1024
NTT = T // 128
C = D // 128
EPS = 1e-5
MEGAS = [2560, 2560, 2560, 3328]
QCH = 1024

ENGS = ("sp", "ve", "ac", "pe", "po")
SKIP_SAME_ENGINE = False


def ilv_index(DD):
    idx = np.empty(DD, dtype=np.int64)
    for g0 in range(0, DD, 512):
        gs = min(512, DD - g0)
        nb = gs // 16
        j = np.arange(gs)
        idx[g0:g0 + gs] = g0 + (j % nb) * 16 + (j // nb)
    return idx


class Plan:
    def __init__(self):
        self.steps = {e: [] for e in ENGS}
        self.counts = {e: 0 for e in ENGS}
        self.lastwait = {}

    def emit(self, eng, fn, reads=(), inc=1):
        waits = []
        for tok in reads:
            if tok is None:
                continue
            peng, pcnt, pinc = tok
            if SKIP_SAME_ENGINE and peng == eng and pinc == 1:
                continue
            key = (eng, peng)
            if pcnt > self.lastwait.get(key, -1):
                self.lastwait[key] = pcnt
                waits.append((peng, pcnt))
        self.counts[eng] += inc
        tok = (eng, self.counts[eng], inc)
        self.steps[eng].append((tuple(waits), fn, inc))
        return tok


class Slot:
    __slots__ = ("ready", "readers")

    def __init__(self):
        self.ready = None
        self.readers = []


def build_nc():
    nc = bass.Bass("TRN2", target_bir_lowering=False, debug=False)
    p = Plan()

    x_tok = nc.dram_tensor("x_tok", [T, D], F32, kind="ExternalInput").ap()
    x_tokU = nc.dram_tensor("x_tokU", [T, D], F32, kind="ExternalInput").ap()
    watt = nc.dram_tensor("watt", [D, D], F32, kind="ExternalInput").ap()
    mask_r = nc.dram_tensor("mask_r", [128, 128, 1024], I32, kind="ExternalInput").ap()
    bias_in = nc.dram_tensor("bias_in", [1, D], F32, kind="ExternalInput").ap()
    wg_sh = nc.dram_tensor("wg_sh", [SH_WG, D], F32, kind="ExternalInput").ap()
    wu_sh = nc.dram_tensor("wu_sh", [SH_WG, D], F32, kind="ExternalInput").ap()
    wd_sh = nc.dram_tensor("wd_sh", [SH_WD, H], F32, kind="ExternalInput").ap()
    scale_sh = nc.dram_tensor("scale_sh", [128, 4], F32, kind="ExternalInput").ap()
    lrA_in = nc.dram_tensor("lrA_in", [128, C * 64], F32, kind="ExternalInput").ap()
    lrB_in = nc.dram_tensor("lrB_in", [64, D], F32, kind="ExternalInput").ap()
    out_d = nc.dram_tensor("out", [T, D], F32, kind="ExternalOutput").ap()

    xnfm_d = nc.dram_tensor("xnfm_d", [C, 128, T], BF16, kind="Internal").ap()
    xqfm_d = nc.dram_tensor("xqfm_d", [C, 128, T], BF16, kind="Internal").ap()
    s_d = nc.dram_tensor("s_d", [T, D], BF16, kind="Internal").ap()
    acc_d = nc.dram_tensor("acc_d", [T, D], F32, kind="Internal").ap()
    bmT_d = nc.dram_tensor("bmT_d", [128, 128], F32, kind="Internal").ap()
    wgq_sh = nc.dram_tensor("wgq_sh", [SH_WG, D], BF16, kind="Internal")
    wuq_sh = nc.dram_tensor("wuq_sh", [SH_WG, D], BF16, kind="Internal")
    wdqT_sh = nc.dram_tensor("wdqT_sh", [H, SH_WD], BF16, kind="Internal")
    wgq_d = nc.dram_tensor("wgq_d", [HP, D], BF16, kind="Internal", addr_space="Shared")
    wuq_d = nc.dram_tensor("wuq_d", [HP, D], BF16, kind="Internal", addr_space="Shared")
    wdT_g = nc.dram_tensor("wdT_g", [NCORES, H, SH_WD], BF16, kind="Internal",
                           addr_space="Shared")

    ctx = []

    def sb(name, shape, dt):
        t = nc.sbuf_tensor(name, shape, dt)
        ctx.append(t)
        return t.__enter__()

    def ps_(name, shape):
        t = nc.psum_tensor(name, shape, F32)
        ctx.append(t)
        return t.__enter__()

    def sem(name):
        t = nc.semaphore(name)
        ctx.append(t)
        return t.__enter__()

    arA = sb("arA", [128, D], F32)
    arB = sb("arB", [128, D], F32)
    arXN = sb("arXN", [128, D], F32)
    q_t = sb("q_t", [128, QCH], F32)
    q_a = sb("q_a", [128, QCH], F32)
    q_k1 = sb("q_k1", [128, QCH], I32)
    q_k2 = sb("q_k2", [128, QCH], I32)
    q_u2 = sb("q_u2", [128, QCH], BF16)
    q_sg = sb("q_sg", [128, QCH], BF16)
    q_m = sb("q_m", [128, D // 16], F32)
    q_rm = sb("q_rm", [128, D // 16], F32)
    q_m12 = sb("q_m12", [128, D // 16], F32)
    wq_tok = sb("wq_tok", [128, D], BF16)
    slab = sb("slab", [128, 16384], BF16)
    wm = [sb(f"wm{i}", [128, 512], BF16) for i in range(2)]
    wa_raw = [sb(f"wa_raw{i}", [128, 512], F32) for i in range(2)]
    xs_st = [sb(f"xs_st{i}", [128, T], BF16) for i in range(2)]
    ev_st = [sb(f"ev_st{i}", [128, 512], BF16) for i in range(2)]
    KMAX = max(MEGAS) // 128
    hq_meg = sb("hq_meg", [128, KMAX, T], BF16)
    bmT_exp = hq_meg[:, 0:4, :].rearrange("p k (a b) -> p (k a) b", b=128)
    h_f = sb("h_f", [128, 512], F32)
    hq_tok = sb("hq_tok", [128, 512], BF16)
    dstage = [sb(f"dstage{i}", [128, 512], F32) for i in range(2)]
    lrA_sb = sb("lrA_sb", [128, C * 64], BF16)
    lrBB = sb("lrBB", [128, D + 256], BF16)
    lrB_sb = lrBB[0:64, 0:D]
    bias_bf = lrBB[64:65, 0:D]
    ones_bf = lrBB[64:65, D:D + 128]
    u_sb = sb("u_sb", [128, NTT * 128], BF16)
    uT = sb("uT", [128, NTT, 128], BF16)
    scale_sb = sb("scale_sb", [128, 4], F32)
    st = {k: sb("st_" + k, [128, 1], F32)
          for k in ("s1", "s2", "mean", "ex2", "m2", "var", "sd", "rstd")}
    bm_i = sb("bm_i", [128, 128], F32)
    bm_b = sb("bm_b", [128, 128], F32)

    q_a_i = q_a[:].bitcast(I32)
    arB_bf = arB[:].bitcast(BF16)[:, 0:D]
    mk_raw = slab[:, 0:8192].bitcast(I32).rearrange("p (a b) -> p a b", b=1024)
    trst = slab[:].rearrange("p (c b) -> p c b", b=128)
    wgq_view = slab[:].rearrange("p (c r b) -> p c r b", r=4, b=128)
    wdq_dir = slab[:].rearrange("p (k f) -> p k f", f=512)
    sz1 = arXN[:].rearrange("p (t f) -> p t f", f=512)

    psb = [ps_(f"psb{i}", [128, 512]) for i in range(8)]

    sems = {e: sem("sem_" + e) for e in ENGS}

    slots = {}

    def slot(obj, tag=""):
        key = (id(obj), tag)
        if key not in slots:
            slots[key] = Slot()
        return slots[key]

    def E(eng, fn, reads=(), writes=(), inc=1):
        toks = []
        for r in reads:
            if isinstance(r, Slot):
                if r.ready is not None:
                    toks.append(r.ready)
            elif r is not None:
                toks.append(r)
        for w in writes:
            toks.extend(w.readers)
            if w.ready is not None:
                toks.append(w.ready)
        tok = p.emit(eng, fn, toks, inc)
        for r in reads:
            if isinstance(r, Slot):
                r.readers.append(tok)
        for w in writes:
            w.ready = tok
            w.readers = []
        return tok

    sl_arA, sl_arB, sl_arXN = slot(arA), slot(arB), slot(arXN)
    sl_wq, sl_slab = slot(wq_tok), slot(slab)
    sl_ps = [slot(b) for b in psb]
    sl_wa = [slot(w) for w in wa_raw]
    sl_wm = [slot(w) for w in wm]
    sl_xs = [slot(x) for x in xs_st]
    sl_ev = [slot(x) for x in ev_st]
    sl_dst = [slot(d) for d in dstage]
    sl_hf, sl_hqt = slot(h_f), slot(hq_tok)
    sl_usb, sl_uT = slot(u_sb), slot(uT)

    q_slots = {name: Slot() for name in
               ("t", "a", "k1", "k2", "u2", "sg", "m", "rm", "m12")}

    def emit_qdq(src_ap, dst_ap, F, tail256, scale_col, pass_views,
                 src_slot, dst_slot, extra_reads=()):
        nfull = (F - 256) // 512 if tail256 else F // 512
        nb_total = F // 16
        reads0 = [src_slot] + list(extra_reads)
        sm, srm, sm12 = q_slots["m"], q_slots["rm"], q_slots["m12"]
        if nfull > 0:
            E("ve", lambda v, nf=nfull: v.tensor_reduce(
                out=q_m[:, 0:nf * 32],
                in_=src_ap(0, nf * 512).rearrange("p (g k b) -> p g k b", k=16, b=32)
                    .transpose([0, 1, 3, 2]),
                axis=AXIS.X, op=ALU.max, apply_absolute_value=True),
              reads=reads0, writes=[sm])
        if tail256:
            E("ve", lambda v, g0=nfull * 512, nf=nfull: v.tensor_reduce(
                out=q_m[:, nf * 32:nf * 32 + 16],
                in_=src_ap(g0, 256).rearrange("p (k b) -> p k b", k=16, b=16)
                    .transpose([0, 2, 1]),
                axis=AXIS.X, op=ALU.max, apply_absolute_value=True),
              reads=reads0, writes=[sm])
        E("ve", lambda v, n=nb_total: v.tensor_scalar_max(q_rm[:, 0:n], q_m[:, 0:n], 1e-30),
          reads=[sm], writes=[srm])
        E("ve", lambda v, n=nb_total: v.reciprocal(out=q_rm[:, 0:n], in_=q_rm[:, 0:n]),
          reads=[srm], writes=[srm])
        if scale_col is None:
            E("ve", lambda v, n=nb_total: v.tensor_scalar_mul(
                q_m12[:, 0:n], q_m[:, 0:n], 1.0 / 12.0),
              reads=[sm], writes=[sm12])
        else:
            E("ve", lambda v, n=nb_total, sc=scale_col: v.tensor_scalar(
                out=q_m12[:, 0:n], in0=q_m[:, 0:n],
                scalar1=1.0 / 12.0, scalar2=sc, op0=ALU.mult, op1=ALU.mult),
              reads=[sm] + list(extra_reads), writes=[sm12])

        off = 0
        while off < F:
            ch = min(QCH, F - off)
            pieces = []
            o = off
            while o < off + ch:
                if tail256 and o >= nfull * 512:
                    pieces.append((o, 256))
                    o += 256
                else:
                    w = min(off + ch, nfull * 512 if tail256 else F) - o
                    pieces.append((o, w))
                    o += w
            st_, sa, sk1, sk2, su2, ssg = (
                q_slots["t"], q_slots["a"], q_slots["k1"], q_slots["k2"],
                q_slots["u2"], q_slots["sg"])
            for (o, gs) in pieces:
                if gs >= 512:
                    ng = gs // 512
                    vi = src_ap(o, gs).rearrange("p (g k b) -> p g k b", k=16, b=32)
                    vr = (q_rm[:, o // 16:(o + gs) // 16]
                          .rearrange("p (g b) -> p g b", b=32)
                          .unsqueeze(2).broadcast_to([128, ng, 16, 32]))
                    vt = (q_t[:, (o - off):(o - off) + gs]
                          .rearrange("p (g k b) -> p g k b", k=16, b=32))
                else:
                    vi = src_ap(o, gs).rearrange("p (k b) -> p k b", k=16, b=16)
                    vr = (q_rm[:, o // 16:o // 16 + 16]
                          .unsqueeze(1).broadcast_to([128, 16, 16]))
                    vt = (q_t[:, (o - off):(o - off) + gs]
                          .rearrange("p (k b) -> p k b", k=16, b=16))
                E("ve", lambda v, a=vi, b=vr, c=vt: v.tensor_tensor(
                    out=c, in0=a, in1=b, op=ALU.mult),
                  reads=reads0 + [srm], writes=[st_])
            E("ac", lambda s, c=ch: s.activation(out=q_a[:, 0:c], in_=q_t[:, 0:c],
                                                 func=ACT.Abs),
              reads=[st_], writes=[sa])
            E("ac", lambda s, c=ch: s.sign(out=q_sg[:, 0:c], in_=q_t[:, 0:c]),
              reads=[st_], writes=[ssg])
            E("ve", lambda v, c=ch: v.tensor_scalar(
                out=q_k1[:, 0:c], in0=q_a[:, 0:c], scalar1=12.0, scalar2=4.0,
                op0=ALU.mult, op1=ALU.min),
              reads=[sa], writes=[sk1])
            E("ve", lambda v, c=ch: v.tensor_scalar(
                out=q_t[:, 0:c], in0=q_a[:, 0:c], scalar1=2.0 / 3.0,
                scalar2=1.0 / 3.0, op0=ALU.min, op1=ALU.max),
              reads=[sa], writes=[st_])
            E("ve", lambda v, c=ch: v.tensor_scalar(
                out=q_k2[:, 0:c], in0=q_t[:, 0:c], scalar1=6.0, scalar2=-2.0,
                op0=ALU.mult, op1=ALU.add),
              reads=[st_], writes=[sk2])
            E("ve", lambda v, c=ch: v.tensor_scalar(
                out=q_a_i[:, 0:c], in0=q_a[:, 0:c], scalar1=5.0 / 6.0,
                scalar2=None, op0=ALU.is_gt),
              reads=[sa], writes=[sa])
            E("ve", lambda v, c=ch: v.scalar_tensor_tensor(
                out=q_k1[:, 0:c], in0=q_k2[:, 0:c], scalar=2, in1=q_k1[:, 0:c],
                op0=ALU.mult, op1=ALU.add),
              reads=[sk2, sk1], writes=[sk1])
            E("ve", lambda v, c=ch: v.scalar_tensor_tensor(
                out=q_k2[:, 0:c], in0=q_a_i[:, 0:c], scalar=4, in1=q_k1[:, 0:c],
                op0=ALU.mult, op1=ALU.add),
              reads=[sa, sk1], writes=[sk2])
            E("ve", lambda v, c=ch: v.tensor_tensor(
                out=q_u2[:, 0:c], in0=q_k2[:, 0:c], in1=q_sg[:, 0:c], op=ALU.mult),
              reads=[sk2, ssg], writes=[su2])
            for (o, gs) in pieces:
                if gs >= 512:
                    ng = gs // 512
                    vu = (q_u2[:, (o - off):(o - off) + gs]
                          .rearrange("p (g k b) -> p g k b", k=16, b=32))
                    vm = (q_m12[:, o // 16:(o + gs) // 16]
                          .rearrange("p (g b) -> p g b", b=32)
                          .unsqueeze(2).broadcast_to([128, ng, 16, 32]))
                    vd = dst_ap(o, gs).rearrange("p (g k b) -> p g k b", k=16, b=32)
                else:
                    vu = (q_u2[:, (o - off):(o - off) + gs]
                          .rearrange("p (k b) -> p k b", k=16, b=16))
                    vm = (q_m12[:, o // 16:o // 16 + 16]
                          .unsqueeze(1).broadcast_to([128, 16, 16]))
                    vd = dst_ap(o, gs).rearrange("p (k b) -> p k b", k=16, b=16)
                E("ve", lambda v, a=vu, b=vm, c=vd: v.tensor_tensor(
                    out=c, in0=a, in1=b, op=ALU.mult),
                  reads=[su2, sm12], writes=[dst_slot])
            off += ch
        if pass_views is not None:
            sv, dv = pass_views
            if scale_col is None:
                E("ve", lambda v, a=sv, b=dv: v.tensor_copy(out=b, in_=a),
                  reads=reads0, writes=[dst_slot])
            else:
                E("ve", lambda v, a=sv, b=dv, sc=scale_col: v.tensor_scalar_mul(b, a, sc),
                  reads=reads0, writes=[dst_slot])

    def upass_views(src, dst):
        sv = src[:, 3584:4096].rearrange("p (k b) -> p k b", b=32)[:, :, 24:32]
        dv = dst[:, 3584:4096].rearrange("p (k b) -> p k b", b=32)[:, :, 24:32]
        return sv, dv

    sl_lrA, sl_lrB = slot(lrA_sb), slot(lrB_sb)
    sl_bias, sl_scale, sl_ones = slot(bias_bf), slot(scale_sb), slot(ones_bf)
    E("po", lambda g: g.dma_start(out=lrA_sb[:], in_=lrA_in[:, :]), writes=[sl_lrA], inc=16)
    E("po", lambda g: g.dma_start(out=lrB_sb, in_=lrB_in[:, :]), writes=[sl_lrB], inc=16)
    E("po", lambda g: g.dma_start(out=bias_bf, in_=bias_in[:, :]), writes=[sl_bias], inc=16)
    E("po", lambda g: g.dma_start(out=scale_sb[:], in_=scale_sh[:, :]), writes=[sl_scale], inc=16)
    E("ve", lambda v: v.memset(ones_bf, 1.0), writes=[sl_ones])

    sl_mk, sl_bmi, sl_bmb = slot(slab, "mk"), slot(bm_i), slot(bm_b)
    for kc in range(32):
        E("sp", lambda s, kc=kc: s.dma_start(
            out=mk_raw[:], in_=mask_r[:, kc * 4:(kc + 1) * 4, :]),
          writes=[sl_slab], inc=16)
        E("ve", lambda v, kc=kc: v.tensor_reduce(
            out=bm_i[:, kc * 4:(kc + 1) * 4], in_=mk_raw[:],
            axis=AXIS.X, op=ALU.add),
          reads=[sl_slab], writes=[sl_bmi])
    E("ve", lambda v: v.tensor_scalar(out=bm_b[:], in0=bm_i[:], scalar1=0, scalar2=None,
                                      op0=ALU.is_gt),
      reads=[sl_bmi], writes=[sl_bmb])
    tok_bmd = None
    for i4 in range(4):
        tok_bmd = E("po", lambda g, i4=i4: g.dma_start(
            out=bmT_d[:, 32 * i4:32 * (i4 + 1)].transpose([1, 0]),
            in_=bm_b[32 * i4:32 * (i4 + 1), :],
            allow_slow_non_contiguous=True),
          reads=[sl_bmb], inc=16)
    sl_bmE = sl_hq = slot(hq_meg, "bmE")
    for cc in range(C):
        for i in range(4):
            E("po", lambda g, cc=cc, i=i: g.dma_start(
                out=bmT_exp[32 * i:32 * (i + 1), cc, :],
                in_=bmT_d[4 * cc + i:4 * cc + i + 1, :].broadcast_to([32, 128])),
              reads=[tok_bmd], writes=[sl_bmE], inc=16)

    sl_wgq_sh, sl_wuq_sh, sl_wdq_sh = slot(wgq_sh), slot(wuq_sh), slot(wdqT_sh)
    cc_toks = {}
    for (src_h, qdst, dslot, key) in ((wg_sh, wgq_sh, sl_wgq_sh, "wg"),
                                      (wu_sh, wuq_sh, sl_wuq_sh, "wu")):
        for t in range(SH_WG // 128):
            row0 = t * 128
            src, ssl = (arA, sl_arA) if t % 2 == 0 else (arB, sl_arB)
            E("sp", lambda s, src=src, src_h=src_h, row0=row0: s.dma_start(
                out=src[:], in_=src_h[row0:row0 + 128, :]),
              writes=[ssl], inc=16)
            emit_qdq(lambda o, n, src=src: src[:, o:o + n],
                     lambda o, n: wq_tok[:, o:o + n],
                     D, False, None, upass_views(src, wq_tok), ssl, sl_wq)
            E("po", lambda g, qdst=qdst, row0=row0: g.dma_start(
                out=qdst.ap()[row0:row0 + 128, :], in_=wq_tok[:]),
              reads=[sl_wq], writes=[dslot], inc=16)
        gdst = wgq_d if key == "wg" else wuq_d
        gsl = slot(gdst)
        cc_toks[key] = E("po", lambda g, qdst=qdst, gdst=gdst: g.collective_compute(
            "AllGather", ALU.bypass, replica_groups=[list(range(NCORES))],
            ins=[qdst.ap().opt()], outs=[gdst.ap().opt()]),
          reads=[dslot], writes=[gsl], inc=1)

    mega_off = 0
    for mi, M in enumerate(MEGAS):
        g256d = (mi == len(MEGAS) - 1)
        KMc = M // 128
        for rt in range(4):
            row0 = rt * 128
            src, ssl = (arA, sl_arA) if rt % 2 == 0 else (arB, sl_arB)
            E("sp", lambda s, src=src, row0=row0, mo=mega_off, M=M: s.dma_start(
                out=src[:, 0:M], in_=wd_sh[row0:row0 + 128, mo:mo + M]),
              writes=[ssl], inc=16)
            pv = None
            if g256d:
                base = 10752 - mega_off
                sv = src[:, base:base + 256].rearrange(
                    "p (k b) -> p k b", b=16)[:, :, 8:16]
                dv = wq_tok[:, base:base + 256].rearrange(
                    "p (k b) -> p k b", b=16)[:, :, 8:16]
                pv = (sv, dv)
            scol = scale_sb[:, rt:rt + 1]
            emit_qdq(lambda o, n, src=src: src[:, o:o + n],
                     lambda o, n: wq_tok[:, o:o + n],
                     M, g256d, scol, pv, ssl, sl_wq, extra_reads=[sl_scale])
            E("sp", lambda s, KMc=KMc, M=M: s.dma_start(
                out=trst[:, 0:KMc, :], in_=wq_tok[:, 0:M], transpose=True),
              reads=[sl_wq], writes=[sl_slab], inc=16)
            E("po", lambda g, mo=mega_off, M=M, row0=row0, KMc=KMc: g.dma_start(
                out=wdqT_sh.ap()[mo:mo + M, row0:row0 + 128]
                    .rearrange("(k p) q -> p k q", p=128),
                in_=trst[:, 0:KMc, :]),
              reads=[sl_slab], writes=[sl_wdq_sh], inc=16)
        mega_off += M
    gsl_wd = slot(wdT_g)
    cc_toks["wd"] = E("po", lambda g: g.collective_compute(
        "AllGather", ALU.bypass, replica_groups=[list(range(NCORES))],
        ins=[wdqT_sh.ap().opt()], outs=[wdT_g.ap().opt()]),
      reads=[sl_wdq_sh], writes=[gsl_wd], inc=1)

    uview = u_sb[:].rearrange("p (tt f) -> p tt f", f=128)
    E("ve", lambda v: v.memset(u_sb[:], 0.0), writes=[sl_usb])

    for tt in range(NTT):
        r0 = tt * 128
        E("sp", lambda s, r0=r0: s.dma_start(out=arA[:], in_=x_tok[r0:r0 + 128, :]),
          writes=[sl_arA], inc=16)
        E("ve", lambda v: v.reduce_sum(out=st["s1"][:], in_=arA[:], axis=AXIS.X),
          reads=[sl_arA], writes=[slot(st["s1"])])
        E("ac", lambda s: s.activation(out=arXN[:], in_=arA[:], func=ACT.Square,
                                       accum_out=st["s2"][:]),
          reads=[sl_arA], writes=[sl_arXN, slot(st["s2"])])
        E("ve", lambda v: v.tensor_scalar_mul(st["mean"][:], st["s1"][:], 1.0 / D),
          reads=[slot(st["s1"])], writes=[slot(st["mean"])])
        E("ve", lambda v: v.tensor_scalar_mul(st["ex2"][:], st["s2"][:], 1.0 / D),
          reads=[slot(st["s2"])], writes=[slot(st["ex2"])])
        E("ve", lambda v: v.tensor_tensor(out=st["m2"][:], in0=st["mean"][:],
                                          in1=st["mean"][:], op=ALU.mult),
          reads=[slot(st["mean"])], writes=[slot(st["m2"])])
        E("ve", lambda v: v.tensor_tensor(out=st["var"][:], in0=st["ex2"][:],
                                          in1=st["m2"][:], op=ALU.subtract),
          reads=[slot(st["ex2"]), slot(st["m2"])], writes=[slot(st["var"])])
        E("ve", lambda v: v.tensor_scalar_add(st["var"][:], st["var"][:], EPS),
          reads=[slot(st["var"])], writes=[slot(st["var"])])
        E("ac", lambda s: s.activation(out=st["sd"][:], in_=st["var"][:], func=ACT.Sqrt),
          reads=[slot(st["var"])], writes=[slot(st["sd"])])
        E("ve", lambda v: v.reciprocal(out=st["rstd"][:], in_=st["sd"][:]),
          reads=[slot(st["sd"])], writes=[slot(st["rstd"])])
        E("ve", lambda v: v.tensor_scalar(out=arXN[:], in0=arA[:], scalar1=st["mean"][:],
                                          scalar2=st["rstd"][:], op0=ALU.subtract,
                                          op1=ALU.mult),
          reads=[sl_arA, slot(st["mean"]), slot(st["rstd"])], writes=[sl_arXN])
        E("ve", lambda v: v.tensor_copy(out=wq_tok[:], in_=arXN[:]),
          reads=[sl_arXN], writes=[sl_wq])
        E("sp", lambda s: s.dma_start(out=trst[:, 0:C, :], in_=wq_tok[:], transpose=True),
          reads=[sl_wq], writes=[sl_slab], inc=16)
        E("po", lambda g, r0=r0: g.dma_start(
            out=xnfm_d[:, :, r0:r0 + 128].transpose([1, 0, 2]), in_=trst[:, 0:C, :]),
          reads=[sl_slab], inc=16)
        lrA_v = lrA_sb[:].rearrange("p (c f) -> p c f", f=64)
        for cc in range(C):
            E("pe", lambda t, cc=cc: t.matmul(
                psb[0][0:128, 0:64], trst[:, cc, :], lrA_v[:, cc, :],
                start=(cc == 0), stop=(cc == C - 1)),
              reads=[sl_slab, sl_lrA],
              writes=([sl_ps[0]] if cc in (0, C - 1) else []))
        E("ve", lambda v, tt=tt: v.tensor_copy(out=uview[:, tt, 0:64],
                                               in_=psb[0][0:128, 0:64]),
          reads=[sl_ps[0]], writes=[sl_usb])
        E("sp", lambda s, r0=r0: s.dma_start(out=arB[:], in_=x_tokU[r0:r0 + 128, :]),
          writes=[sl_arB], inc=16)
        E("ve", lambda v: v.tensor_scalar(out=arB[:], in0=arB[:], scalar1=st["mean"][:],
                                          scalar2=st["rstd"][:], op0=ALU.subtract,
                                          op1=ALU.mult),
          reads=[sl_arB, slot(st["mean"]), slot(st["rstd"])], writes=[sl_arB])
        emit_qdq(lambda o, n: arB[:, o:o + n], lambda o, n: wq_tok[:, o:o + n],
                 D, False, None, upass_views(arB, wq_tok), sl_arB, sl_wq)
        E("sp", lambda s: s.dma_start(out=trst[:, 0:C, :], in_=wq_tok[:], transpose=True),
          reads=[sl_wq], writes=[sl_slab], inc=16)
        E("po", lambda g, r0=r0: g.dma_start(
            out=xqfm_d[:, :, r0:r0 + 128].transpose([1, 0, 2]), in_=trst[:, 0:C, :]),
          reads=[sl_slab], inc=16)

    E("sp", lambda s: s.dma_start(out=uT[:], in_=u_sb[:], transpose=True),
      reads=[sl_usb], writes=[sl_uT], inc=16)

    for nf in range(8):
        for cc in range(C):
            ib = cc % 2
            E("sp", lambda s, nf=nf, cc=cc, ib=ib: s.dma_start(
                out=wa_raw[ib][:],
                in_=watt[cc * 128:(cc + 1) * 128, nf * 512:(nf + 1) * 512]),
              writes=[sl_wa[ib]], inc=16)
            E("ve", lambda v, nf=nf, ib=ib, cc=cc: v.tensor_tensor(
                out=wm[ib][:].rearrange("p (nb w) -> p nb w", w=32),
                in0=wa_raw[ib][:].rearrange("p (nb w) -> p nb w", w=32),
                in1=bmT_exp[:, cc, nf * 16:(nf + 1) * 16].unsqueeze(2)
                    .broadcast_to([128, 16, 32]),
                op=ALU.mult),
              reads=[sl_wa[ib], sl_bmE], writes=[sl_wm[ib]])
            E("ac", lambda s, cc=cc, ib=ib: s.dma_start(
                out=xs_st[ib][:], in_=xnfm_d[cc, :, :]),
              writes=[sl_xs[ib]], inc=16)
            for tt in range(NTT):
                E("pe", lambda t, cc=cc, tt=tt, ib=ib: t.matmul(
                    psb[tt][:, :], xs_st[ib][:, tt * 128:(tt + 1) * 128], wm[ib][:],
                    start=(cc == 0), stop=False),
                  reads=[sl_xs[ib], sl_wm[ib]],
                  writes=([sl_ps[tt]] if cc == 0 else []))
        for tt in range(NTT):
            E("pe", lambda t, nf=nf, tt=tt: t.matmul(
                psb[tt][:, :], ones_bf, bias_bf[:, nf * 512:(nf + 1) * 512],
                start=False, stop=True),
              reads=[sl_ones, sl_bias], writes=[sl_ps[tt]])
        for tt in range(NTT):
            ib = tt % 2
            E("ve", lambda v, tt=tt, ib=ib: v.tensor_copy(out=ev_st[ib][:],
                                                          in_=psb[tt][:, :]),
              reads=[sl_ps[tt]], writes=[sl_ev[ib]])
            E("po", lambda g, nf=nf, tt=tt, ib=ib: g.dma_start(
                out=s_d[tt * 128:(tt + 1) * 128, nf * 512:(nf + 1) * 512],
                in_=ev_st[ib][:]),
              reads=[sl_ev[ib]], inc=16)

    mega_off = 0
    for mi, M in enumerate(MEGAS):
        KM = M // 128
        hf_tiles = []
        o = 0
        while o < M:
            ts = min(512, M - o)
            hf_tiles.append((o, ts))
            o += ts
        for (ho, hts) in hf_tiles:
            g256 = (mi == len(MEGAS) - 1 and mega_off + ho + hts > H - 256)
            for zi, (wsrc, cck) in ((0, (wgq_d, "wg")), (1, (wuq_d, "wu"))):
                for rb in range(hts // 128):
                    row0 = mega_off + ho + rb * 128
                    if rb % 2 == 0:
                        buf, bsl = wq_tok[:], sl_wq
                    else:
                        buf, bsl = arB_bf, sl_arB
                    E("ac", lambda s, wsrc=wsrc, row0=row0, buf=buf, ck=cck: s.dma_start(
                        out=buf, in_=wsrc.ap()[row0:row0 + 128, :]),
                      reads=[cc_toks[cck]], writes=[bsl], inc=16)
                    E("sp", lambda s, rb=rb, buf=buf: s.dma_start(
                        out=wgq_view[:, :, rb, :], in_=buf, transpose=True),
                      reads=[bsl], writes=[sl_slab], inc=16)
                for cc in range(C):
                    ib = cc % 2
                    E("ac", lambda s, cc=cc, ib=ib: s.dma_start(
                        out=xs_st[ib][:], in_=xqfm_d[cc, :, :]),
                      writes=[sl_xs[ib]], inc=16)
                    for tt in range(NTT):
                        E("pe", lambda t, cc=cc, ib=ib, tt=tt, hts=hts: t.matmul(
                            psb[tt][:, 0:hts],
                            xs_st[ib][:, tt * 128:(tt + 1) * 128],
                            slab[:].rearrange("p (c f) -> p c f", f=512)[:, cc, 0:hts],
                            start=(cc == 0), stop=(cc == C - 1)),
                          reads=[sl_xs[ib], sl_slab],
                          writes=([sl_ps[tt]] if cc in (0, C - 1) else []))
                if zi == 0:
                    for tt in range(NTT):
                        E("ac", lambda s, tt=tt, hts=hts: s.activation(
                            out=sz1[:, tt, 0:hts], in_=psb[tt][:, 0:hts], func=ACT.Silu),
                          reads=[sl_ps[tt]], writes=[sl_arXN])
            for tt in range(NTT):
                E("ve", lambda v, tt=tt, hts=hts: v.tensor_tensor(
                    out=h_f[:, 0:hts], in0=sz1[:, tt, 0:hts], in1=psb[tt][:, 0:hts],
                    op=ALU.mult),
                  reads=[sl_arXN, sl_ps[tt]], writes=[sl_hf])
                pv = None
                if g256:
                    base = 10752 - (mega_off + ho)
                    sv = h_f[:, base:base + 256].rearrange(
                        "p (k b) -> p k b", b=16)[:, :, 8:16]
                    dv = hq_tok[:, base:base + 256].rearrange(
                        "p (k b) -> p k b", b=16)[:, :, 8:16]
                    pv = (sv, dv)
                emit_qdq(lambda o_, n_: h_f[:, o_:o_ + n_],
                         lambda o_, n_: hq_tok[:, o_:o_ + n_],
                         hts, g256, None, pv, sl_hf, sl_hqt)
                E("sp", lambda s, ho=ho, tt=tt, hts=hts: s.dma_start(
                    out=hq_meg[:, ho // 128:(ho + hts) // 128, tt * 128:(tt + 1) * 128],
                    in_=hq_tok[:, 0:hts], transpose=True),
                  reads=[sl_hqt], writes=[sl_hq], inc=16)
        for df in range(8):
            for k in range(KM):
                E("sp", lambda s, df=df, k=k, mo=mega_off: s.dma_start(
                    out=wdq_dir[:, k, :],
                    in_=wdT_g.ap()[df, mo + k * 128:mo + (k + 1) * 128, :]),
                  reads=[cc_toks["wd"]], writes=[sl_slab], inc=16)
            for tt in range(NTT):
                for k in range(KM):
                    last = (k == KM - 1 and mi != 0)
                    E("pe", lambda t, k=k, tt=tt: t.matmul(
                        psb[tt][:, :], hq_meg[:, k, tt * 128:(tt + 1) * 128],
                        wdq_dir[:, k, :],
                        start=(k == 0), stop=(k == KM - 1 and mi != 0)),
                      reads=[sl_hq, sl_slab],
                      writes=([sl_ps[tt]] if (k == 0 or last) else []))
                if mi == 0:
                    E("pe", lambda t, tt=tt, df=df: t.matmul(
                        psb[tt][:, :], uT[0:64, tt, :],
                        lrB_sb[:, df * 512:(df + 1) * 512],
                        start=False, stop=True),
                      reads=[sl_uT, sl_lrB], writes=[sl_ps[tt]])
                ib = tt % 2
                E("ve", lambda v, tt=tt, ib=ib: v.tensor_copy(
                    out=dstage[ib][:], in_=psb[tt][:, :]),
                  reads=[sl_ps[tt]], writes=[sl_dst[ib]])
                E("po", lambda g, tt=tt, df=df, ib=ib, mi=mi: g.dma_start(
                    out=acc_d[tt * 128:(tt + 1) * 128, df * 512:(df + 1) * 512],
                    in_=dstage[ib][:],
                    accum_op=(ALU.bypass if mi == 0 else ALU.add)),
                  reads=[sl_dst[ib]], inc=16)
        mega_off += M

    for tt in range(NTT):
        r0 = tt * 128
        E("sp", lambda s, r0=r0: s.dma_start(out=arA[:], in_=x_tok[r0:r0 + 128, :]),
          writes=[sl_arA], inc=16)
        E("po", lambda g, r0=r0: g.dma_start(out=arB[:], in_=acc_d[r0:r0 + 128, :]),
          writes=[sl_arB], inc=16)
        E("sp", lambda s, r0=r0: s.dma_start(out=wq_tok[:], in_=s_d[r0:r0 + 128, :]),
          writes=[sl_wq], inc=16)
        E("ac", lambda s: s.activation(out=arXN[:], in_=arB[:], func=ACT.Silu),
          reads=[sl_arB], writes=[sl_arXN])
        E("ve", lambda v: v.tensor_tensor(out=arXN[:], in0=arXN[:], in1=wq_tok[:],
                                          op=ALU.mult),
          reads=[sl_arXN, sl_wq], writes=[sl_arXN])
        E("ve", lambda v: v.tensor_tensor(out=arB[:], in0=arXN[:], in1=arA[:],
                                          op=ALU.add),
          reads=[sl_arXN, sl_arA], writes=[sl_arB])
        E("po", lambda g, r0=r0: g.dma_start(out=out_d[r0:r0 + 128, :], in_=arB[:]),
          reads=[sl_arB], inc=16)

    with nc.Block() as block:
        def runner(engname, handle):
            for (waits, fn, inc) in p.steps[engname]:
                for (peng, pcnt) in waits:
                    handle.wait_ge(sems[peng], pcnt)
                fn(handle).then_inc(sems[engname], inc)
            if engname == "po":
                handle.wait_ge(sems["po"], p.counts["po"])

        @block.sync
        def _(h):
            runner("sp", h)

        @block.vector
        def _(h):
            runner("ve", h)

        @block.scalar
        def _(h):
            runner("ac", h)

        @block.tensor
        def _(h):
            runner("pe", h)

        @block.gpsimd
        def _(h):
            runner("po", h)

    for t in reversed(ctx):
        t.__exit__(None, None, None)
    return nc


_NC_CACHE = {}


def kernel(L0_attn_data, L0_attn_mask, L0_attn_weight, L0_attn_bias,
           L0_ffn_w_gate, L0_ffn_w_up, L0_ffn_w_down, L0_ffn_mlp_output_scale,
           L0_ffn_lr_A, L0_ffn_lr_B, L0_ffn_up_reorder_index,
           L0_ffn_down_reorder_index):
    x = np.ascontiguousarray(np.asarray(L0_attn_data, dtype=np.float32).reshape(8192, D))
    up_perm = np.asarray(L0_ffn_up_reorder_index, dtype=np.int64)
    down_perm = np.asarray(L0_ffn_down_reorder_index, dtype=np.int64)
    pU = up_perm[ilv_index(D)]
    hsel = down_perm[ilv_index(H)]

    watt_T = np.ascontiguousarray(np.asarray(L0_attn_weight, dtype=np.float32).T)
    mask_r = np.ascontiguousarray(
        np.asarray(L0_attn_mask, dtype=np.int32)
        .reshape(128, 32, 128, 32).transpose(0, 2, 1, 3).reshape(128, 128, 1024))
    bias_in = np.ascontiguousarray(np.asarray(L0_attn_bias, dtype=np.float32)[None, :])
    wg_hh = np.asarray(L0_ffn_w_gate, dtype=np.float32)[hsel][:, pU]
    wu_hh = np.asarray(L0_ffn_w_up, dtype=np.float32)[hsel][:, pU]
    wgp = np.vstack([wg_hh, np.zeros((HP - H, D), dtype=np.float32)])
    wup = np.vstack([wu_hh, np.zeros((HP - H, D), dtype=np.float32)])
    wd_hh = np.asarray(L0_ffn_w_down, dtype=np.float32)[:, hsel]
    scale_f = np.asarray(L0_ffn_mlp_output_scale, dtype=np.float32)
    lrA_in = np.ascontiguousarray(
        np.asarray(L0_ffn_lr_A, dtype=np.float32).reshape(C, 128, 64)
        .transpose(1, 0, 2).reshape(128, C * 64))
    lrB_in = np.ascontiguousarray(np.asarray(L0_ffn_lr_B, dtype=np.float32))

    if "main" not in _NC_CACHE:
        _NC_CACHE["main"] = build_nc()
    nc = _NC_CACHE["main"]

    in_maps = []
    for ci in range(NCORES):
        xs = np.ascontiguousarray(x[ci * T:(ci + 1) * T, :])
        xsU = np.ascontiguousarray(xs[:, pU])
        in_maps.append({
            "x_tok": xs, "x_tokU": xsU, "watt": watt_T, "mask_r": mask_r,
            "bias_in": bias_in,
            "wg_sh": np.ascontiguousarray(wgp[ci * SH_WG:(ci + 1) * SH_WG]),
            "wu_sh": np.ascontiguousarray(wup[ci * SH_WG:(ci + 1) * SH_WG]),
            "wd_sh": np.ascontiguousarray(wd_hh[ci * SH_WD:(ci + 1) * SH_WD]),
            "scale_sh": np.ascontiguousarray(
                scale_f[ci * SH_WD:(ci + 1) * SH_WD].reshape(4, 128).T),
            "lrA_in": lrA_in, "lrB_in": lrB_in,
        })
    res = run_bass_kernel_spmd(nc, in_maps, core_ids=list(range(NCORES)))
    out = np.concatenate([np.asarray(r["out"]) for r in res.results], axis=0)
    return out.reshape(2, 4096, D)


# version 05
# speedup vs baseline: 1.7284x; 1.0336x over previous
import numpy as np
import concourse.bass as bass
import concourse.mybir as mybir
from concourse.bass_utils import run_bass_kernel_spmd

F32 = mybir.dt.float32
BF16 = mybir.dt.bfloat16
I32 = mybir.dt.int32
ALU = mybir.AluOpType
AXIS = mybir.AxisListType
ACT = mybir.ActivationFunctionType

D = 4096
H = 11008
HP = 11264
SH_WG = HP // 8
SH_WD = D // 8
NCORES = 8
T = 1024
NTT = T // 128
C = D // 128
EPS = 1e-5
MEGAS = [2560, 2560, 2560, 3328]
QCH = 512

ENGS = ("sp", "ve", "ac", "pe", "po")
SKIP_SAME_ENGINE = False


def ilv_index(DD):
    idx = np.empty(DD, dtype=np.int64)
    for g0 in range(0, DD, 512):
        gs = min(512, DD - g0)
        nb = gs // 16
        j = np.arange(gs)
        idx[g0:g0 + gs] = g0 + (j % nb) * 16 + (j // nb)
    return idx


class Plan:
    def __init__(self):
        self.steps = {e: [] for e in ENGS}
        self.counts = {e: 0 for e in ENGS}
        self.lastwait = {}

    def emit(self, eng, fn, reads=(), inc=1):
        waits = []
        for tok in reads:
            if tok is None:
                continue
            peng, pcnt, pinc = tok
            if SKIP_SAME_ENGINE and peng == eng and pinc == 1:
                continue
            key = (eng, peng)
            if pcnt > self.lastwait.get(key, -1):
                self.lastwait[key] = pcnt
                waits.append((peng, pcnt))
        self.counts[eng] += inc
        tok = (eng, self.counts[eng], inc)
        self.steps[eng].append((tuple(waits), fn, inc))
        return tok


class Slot:
    __slots__ = ("ready", "readers")

    def __init__(self):
        self.ready = None
        self.readers = []


def build_nc():
    nc = bass.Bass("TRN2", target_bir_lowering=False, debug=False)
    p = Plan()

    x_tok = nc.dram_tensor("x_tok", [T, D], F32, kind="ExternalInput").ap()
    x_tokU = nc.dram_tensor("x_tokU", [T, D], F32, kind="ExternalInput").ap()
    watt = nc.dram_tensor("watt", [D, D], F32, kind="ExternalInput").ap()
    mask_r = nc.dram_tensor("mask_r", [128, 128, 1024], I32, kind="ExternalInput").ap()
    bias_in = nc.dram_tensor("bias_in", [1, D], F32, kind="ExternalInput").ap()
    wg_sh = nc.dram_tensor("wg_sh", [SH_WG, D], F32, kind="ExternalInput").ap()
    wu_sh = nc.dram_tensor("wu_sh", [SH_WG, D], F32, kind="ExternalInput").ap()
    wd_sh = nc.dram_tensor("wd_sh", [SH_WD, H], F32, kind="ExternalInput").ap()
    scale_sh = nc.dram_tensor("scale_sh", [128, 4], F32, kind="ExternalInput").ap()
    lrA_in = nc.dram_tensor("lrA_in", [128, C * 64], F32, kind="ExternalInput").ap()
    lrB_in = nc.dram_tensor("lrB_in", [64, D], F32, kind="ExternalInput").ap()
    out_d = nc.dram_tensor("out", [T, D], F32, kind="ExternalOutput").ap()

    xnfm_d = nc.dram_tensor("xnfm_d", [C, 128, T], BF16, kind="Internal").ap()
    xqfm_d = nc.dram_tensor("xqfm_d", [C, 128, T], BF16, kind="Internal").ap()
    s_d = nc.dram_tensor("s_d", [T, D], BF16, kind="Internal").ap()
    acc_d = nc.dram_tensor("acc_d", [T, D], F32, kind="Internal").ap()
    bmT_d = nc.dram_tensor("bmT_d", [128, 128], F32, kind="Internal").ap()
    wgq_sh = nc.dram_tensor("wgq_sh", [SH_WG, D], BF16, kind="Internal")
    wuq_sh = nc.dram_tensor("wuq_sh", [SH_WG, D], BF16, kind="Internal")
    wdqT_sh = nc.dram_tensor("wdqT_sh", [H, SH_WD], BF16, kind="Internal")
    wgq_d = nc.dram_tensor("wgq_d", [HP, D], BF16, kind="Internal", addr_space="Shared")
    wuq_d = nc.dram_tensor("wuq_d", [HP, D], BF16, kind="Internal", addr_space="Shared")
    wdT_g = nc.dram_tensor("wdT_g", [NCORES, H, SH_WD], BF16, kind="Internal",
                           addr_space="Shared")

    ctx = []

    def sb(name, shape, dt):
        t = nc.sbuf_tensor(name, shape, dt)
        ctx.append(t)
        return t.__enter__()

    def ps_(name, shape):
        t = nc.psum_tensor(name, shape, F32)
        ctx.append(t)
        return t.__enter__()

    def sem(name):
        t = nc.semaphore(name)
        ctx.append(t)
        return t.__enter__()

    arA = sb("arA", [128, D], F32)
    arB = sb("arB", [128, D], F32)
    arXN = sb("arXN", [128, D], F32)
    q_t = [sb(f"q_t{i}", [128, QCH], F32) for i in range(2)]
    q_a = [sb(f"q_a{i}", [128, QCH], F32) for i in range(2)]
    q_sg = [sb(f"q_sg{i}", [128, QCH], BF16) for i in range(2)]
    q_k1 = sb("q_k1", [128, QCH], I32)
    q_k2 = sb("q_k2", [128, QCH], I32)
    q_m = sb("q_m", [128, D // 16], F32)
    q_rm = sb("q_rm", [128, D // 16], F32)
    wq_tok = sb("wq_tok", [128, D], BF16)
    slab = sb("slab", [128, 16384], BF16)
    wm = [sb(f"wm{i}", [128, 512], BF16) for i in range(2)]
    wa_raw = [sb(f"wa_raw{i}", [128, 512], F32) for i in range(2)]
    xs_st = [sb(f"xs_st{i}", [128, T], BF16) for i in range(2)]
    ev_st = [sb(f"ev_st{i}", [128, 512], BF16) for i in range(2)]
    KMAX = max(MEGAS) // 128
    hq_meg = sb("hq_meg", [128, KMAX, T], BF16)
    bmT_exp = hq_meg[:, 0:4, :].rearrange("p k (a b) -> p (k a) b", b=128)
    hq8 = sb("hq8", [128, D], BF16)
    dstage = [sb(f"dstage{i}", [128, 512], F32) for i in range(2)]
    lrA_sb = sb("lrA_sb", [128, C * 64], BF16)
    lrBB = sb("lrBB", [128, D + 256], BF16)
    lrB_sb = lrBB[0:64, 0:D]
    bias_bf = lrBB[64:65, 0:D]
    ones_bf = lrBB[64:65, D:D + 128]
    u_sb = sb("u_sb", [128, NTT * 128], BF16)
    uT = sb("uT", [128, NTT, 128], BF16)
    scale_sb = sb("scale_sb", [128, 4], F32)
    st = {k: sb("st_" + k, [128, 1], F32)
          for k in ("s1", "s2", "mean", "ex2", "m2", "var", "sd", "rstd")}
    bm_i = sb("bm_i", [128, 128], F32)
    bm_b = sb("bm_b", [128, 128], F32)

    q_a_i = [a[:].bitcast(I32) for a in q_a]
    arB_bf = arB[:].bitcast(BF16)[:, 0:D]
    mk_raw = slab[:, 0:8192].bitcast(I32).rearrange("p (a b) -> p a b", b=1024)
    trst = slab[:].rearrange("p (c b) -> p c b", b=128)
    wgq_view = slab[:].rearrange("p (c r b) -> p c r b", r=4, b=128)
    wdq_dir = slab[:].rearrange("p (k f) -> p k f", f=512)
    sz1 = arXN[:].rearrange("p (t f) -> p t f", f=512)

    psb = [ps_(f"psb{i}", [128, 512]) for i in range(8)]

    sems = {e: sem("sem_" + e) for e in ENGS}

    slots = {}

    def slot(obj, tag=""):
        key = (id(obj), tag)
        if key not in slots:
            slots[key] = Slot()
        return slots[key]

    def E(eng, fn, reads=(), writes=(), inc=1):
        toks = []
        for r in reads:
            if isinstance(r, Slot):
                if r.ready is not None:
                    toks.append(r.ready)
            elif r is not None:
                toks.append(r)
        for w in writes:
            toks.extend(w.readers)
            if w.ready is not None:
                toks.append(w.ready)
        tok = p.emit(eng, fn, toks, inc)
        for r in reads:
            if isinstance(r, Slot):
                r.readers.append(tok)
        for w in writes:
            w.ready = tok
            w.readers = []
        return tok

    sl_arA, sl_arB, sl_arXN = slot(arA), slot(arB), slot(arXN)
    sl_wq, sl_slab = slot(wq_tok), slot(slab)
    sl_ps = [slot(b) for b in psb]
    sl_wa = [slot(w) for w in wa_raw]
    sl_wm = [slot(w) for w in wm]
    sl_xs = [slot(x) for x in xs_st]
    sl_ev = [slot(x) for x in ev_st]
    sl_dst = [slot(d) for d in dstage]
    sl_hq8 = slot(hq8)
    sl_usb, sl_uT = slot(u_sb), slot(uT)

    q_slots = {name: Slot() for name in
               ("t0", "t1", "a0", "a1", "sg0", "sg1", "k1", "k2", "m", "rm")}

    def emit_qdq(src_ap, dst_ap, F, tail256, scale_col, pass_views,
                 src_slot, dst_slot, extra_reads=(), all256=False):
        nfull = 0 if all256 else ((F - 256) // 512 if tail256 else F // 512)
        nb_total = F // 16
        reads0 = [src_slot] + list(extra_reads)
        sm, srm = q_slots["m"], q_slots["rm"]
        if all256:
            E("ve", lambda v: v.tensor_reduce(
                out=q_m[:, 0:nb_total],
                in_=src_ap(0, F).rearrange("p (g k b) -> p g k b", k=16, b=16)
                    .transpose([0, 1, 3, 2]),
                axis=AXIS.X, op=ALU.max, apply_absolute_value=True),
              reads=reads0, writes=[sm])
        else:
            if nfull > 0:
                E("ve", lambda v, nf=nfull: v.tensor_reduce(
                    out=q_m[:, 0:nf * 32],
                    in_=src_ap(0, nf * 512).rearrange("p (g k b) -> p g k b", k=16, b=32)
                        .transpose([0, 1, 3, 2]),
                    axis=AXIS.X, op=ALU.max, apply_absolute_value=True),
                  reads=reads0, writes=[sm])
            if tail256:
                E("ve", lambda v, g0=nfull * 512, nf=nfull: v.tensor_reduce(
                    out=q_m[:, nf * 32:nf * 32 + 16],
                    in_=src_ap(g0, 256).rearrange("p (k b) -> p k b", k=16, b=16)
                        .transpose([0, 2, 1]),
                    axis=AXIS.X, op=ALU.max, apply_absolute_value=True),
                  reads=reads0, writes=[sm])
        E("ve", lambda v, n=nb_total: v.tensor_scalar_max(q_rm[:, 0:n], q_m[:, 0:n], 1e-30),
          reads=[sm], writes=[srm])
        E("ve", lambda v, n=nb_total: v.reciprocal(out=q_rm[:, 0:n], in_=q_rm[:, 0:n]),
          reads=[srm], writes=[srm])
        if scale_col is None:
            E("ve", lambda v, n=nb_total: v.tensor_scalar_mul(
                q_m[:, 0:n], q_m[:, 0:n], 1.0 / 12.0),
              reads=[sm], writes=[sm])
        else:
            E("ve", lambda v, n=nb_total, sc=scale_col: v.tensor_scalar(
                out=q_m[:, 0:n], in0=q_m[:, 0:n],
                scalar1=1.0 / 12.0, scalar2=sc, op0=ALU.mult, op1=ALU.mult),
              reads=[sm] + list(extra_reads), writes=[sm])

        chunks = []
        off = 0
        while off < F:
            ch = min(QCH, F - off)
            pieces = []
            o = off
            while o < off + ch:
                if all256 or (tail256 and o >= nfull * 512):
                    pieces.append((o, 256))
                    o += 256
                else:
                    w = min(off + ch, nfull * 512 if tail256 else F) - o
                    pieces.append((o, w))
                    o += w
            chunks.append((off, ch, pieces))
            off += ch

        def views(o, gs, buf, boff):
            if gs >= 512:
                ng = gs // 512
                vs = src_ap(o, gs).rearrange("p (g k b) -> p g k b", k=16, b=32)
                vr = (q_rm[:, o // 16:(o + gs) // 16]
                      .rearrange("p (g b) -> p g b", b=32)
                      .unsqueeze(2).broadcast_to([128, ng, 16, 32]))
                vm = (q_m[:, o // 16:(o + gs) // 16]
                      .rearrange("p (g b) -> p g b", b=32)
                      .unsqueeze(2).broadcast_to([128, ng, 16, 32]))
                vb = (buf[:, boff:boff + gs]
                      .rearrange("p (g k b) -> p g k b", k=16, b=32))
                vd = dst_ap(o, gs).rearrange("p (g k b) -> p g k b", k=16, b=32)
            else:
                vs = src_ap(o, gs).rearrange("p (k b) -> p k b", k=16, b=16)
                vr = (q_rm[:, o // 16:o // 16 + 16]
                      .unsqueeze(1).broadcast_to([128, 16, 16]))
                vm = (q_m[:, o // 16:o // 16 + 16]
                      .unsqueeze(1).broadcast_to([128, 16, 16]))
                vb = (buf[:, boff:boff + gs]
                      .rearrange("p (k b) -> p k b", k=16, b=16))
                vd = dst_ap(o, gs).rearrange("p (k b) -> p k b", k=16, b=16)
            return vs, vr, vm, vb, vd

        def stage1(ci):
            (off, ch, pieces) = chunks[ci]
            pr = ci % 2
            st_, sa, ssg = q_slots["t%d" % pr], q_slots["a%d" % pr], q_slots["sg%d" % pr]
            for (o, gs) in pieces:
                vs, vr, _, vt, _ = views(o, gs, q_t[pr], o - off)
                E("ve", lambda v, a=vs, b=vr, c=vt: v.tensor_tensor(
                    out=c, in0=a, in1=b, op=ALU.mult),
                  reads=reads0 + [srm], writes=[st_])
            E("ac", lambda s, c=ch, pr=pr: s.activation(
                out=q_a[pr][:, 0:c], in_=q_t[pr][:, 0:c], func=ACT.Abs),
              reads=[st_], writes=[sa])
            E("ac", lambda s, c=ch, pr=pr: s.sign(
                out=q_sg[pr][:, 0:c], in_=q_t[pr][:, 0:c]),
              reads=[st_], writes=[ssg])

        def stage2(ci):
            (off, ch, pieces) = chunks[ci]
            pr = ci % 2
            st_, sa, ssg = q_slots["t%d" % pr], q_slots["a%d" % pr], q_slots["sg%d" % pr]
            sk1, sk2 = q_slots["k1"], q_slots["k2"]
            E("ve", lambda v, c=ch, pr=pr: v.tensor_scalar(
                out=q_k1[:, 0:c], in0=q_a[pr][:, 0:c], scalar1=12.0, scalar2=4.0,
                op0=ALU.mult, op1=ALU.min),
              reads=[sa], writes=[sk1])
            E("ve", lambda v, c=ch, pr=pr: v.tensor_scalar(
                out=q_t[pr][:, 0:c], in0=q_a[pr][:, 0:c], scalar1=2.0 / 3.0,
                scalar2=1.0 / 3.0, op0=ALU.min, op1=ALU.max),
              reads=[sa], writes=[st_])
            E("ve", lambda v, c=ch, pr=pr: v.tensor_scalar(
                out=q_k2[:, 0:c], in0=q_t[pr][:, 0:c], scalar1=6.0, scalar2=-2.0,
                op0=ALU.mult, op1=ALU.add),
              reads=[st_], writes=[sk2])
            E("ve", lambda v, c=ch, pr=pr: v.tensor_scalar(
                out=q_a_i[pr][:, 0:c], in0=q_a[pr][:, 0:c], scalar1=5.0 / 6.0,
                scalar2=None, op0=ALU.is_gt),
              reads=[sa], writes=[sa])
            E("ve", lambda v, c=ch: v.scalar_tensor_tensor(
                out=q_k1[:, 0:c], in0=q_k2[:, 0:c], scalar=2, in1=q_k1[:, 0:c],
                op0=ALU.mult, op1=ALU.add),
              reads=[sk2, sk1], writes=[sk1])
            E("ve", lambda v, c=ch, pr=pr: v.scalar_tensor_tensor(
                out=q_k2[:, 0:c], in0=q_a_i[pr][:, 0:c], scalar=4, in1=q_k1[:, 0:c],
                op0=ALU.mult, op1=ALU.add),
              reads=[sa, sk1], writes=[sk2])
            E("ve", lambda v, c=ch, pr=pr: v.tensor_tensor(
                out=q_sg[pr][:, 0:c], in0=q_k2[:, 0:c], in1=q_sg[pr][:, 0:c],
                op=ALU.mult),
              reads=[sk2, ssg], writes=[ssg])
            for (o, gs) in pieces:
                _, _, vm, vu, vd = views(o, gs, q_sg[pr], o - off)
                E("ve", lambda v, a=vu, b=vm, c=vd: v.tensor_tensor(
                    out=c, in0=a, in1=b, op=ALU.mult),
                  reads=[ssg, sm], writes=[dst_slot])

        stage1(0)
        for ci in range(len(chunks)):
            if ci + 1 < len(chunks):
                stage1(ci + 1)
            stage2(ci)
        if pass_views is not None:
            sv, dv = pass_views
            if scale_col is None:
                E("ve", lambda v, a=sv, b=dv: v.tensor_copy(out=b, in_=a),
                  reads=reads0, writes=[dst_slot])
            else:
                E("ve", lambda v, a=sv, b=dv, sc=scale_col: v.tensor_scalar_mul(b, a, sc),
                  reads=reads0, writes=[dst_slot])

    def upass_views(src, dst):
        sv = src[:, 3584:4096].rearrange("p (k b) -> p k b", b=32)[:, :, 24:32]
        dv = dst[:, 3584:4096].rearrange("p (k b) -> p k b", b=32)[:, :, 24:32]
        return sv, dv

    sl_lrA, sl_lrB = slot(lrA_sb), slot(lrB_sb)
    sl_bias, sl_scale, sl_ones = slot(bias_bf), slot(scale_sb), slot(ones_bf)
    E("po", lambda g: g.dma_start(out=lrA_sb[:], in_=lrA_in[:, :]), writes=[sl_lrA], inc=16)
    E("po", lambda g: g.dma_start(out=lrB_sb, in_=lrB_in[:, :]), writes=[sl_lrB], inc=16)
    E("po", lambda g: g.dma_start(out=bias_bf, in_=bias_in[:, :]), writes=[sl_bias], inc=16)
    E("po", lambda g: g.dma_start(out=scale_sb[:], in_=scale_sh[:, :]), writes=[sl_scale], inc=16)
    E("ve", lambda v: v.memset(ones_bf, 1.0), writes=[sl_ones])

    sl_mk, sl_bmi, sl_bmb = slot(slab, "mk"), slot(bm_i), slot(bm_b)
    for kc in range(32):
        E("sp", lambda s, kc=kc: s.dma_start(
            out=mk_raw[:], in_=mask_r[:, kc * 4:(kc + 1) * 4, :]),
          writes=[sl_slab], inc=16)
        E("ve", lambda v, kc=kc: v.tensor_reduce(
            out=bm_i[:, kc * 4:(kc + 1) * 4], in_=mk_raw[:],
            axis=AXIS.X, op=ALU.add),
          reads=[sl_slab], writes=[sl_bmi])
    E("ve", lambda v: v.tensor_scalar(out=bm_b[:], in0=bm_i[:], scalar1=0, scalar2=None,
                                      op0=ALU.is_gt),
      reads=[sl_bmi], writes=[sl_bmb])
    tok_bmd = None
    for i4 in range(4):
        tok_bmd = E("po", lambda g, i4=i4: g.dma_start(
            out=bmT_d[:, 32 * i4:32 * (i4 + 1)].transpose([1, 0]),
            in_=bm_b[32 * i4:32 * (i4 + 1), :],
            allow_slow_non_contiguous=True),
          reads=[sl_bmb], inc=16)
    sl_bmE = sl_hq = slot(hq_meg, "bmE")
    for cc in range(C):
        for i in range(4):
            E("po", lambda g, cc=cc, i=i: g.dma_start(
                out=bmT_exp[32 * i:32 * (i + 1), cc, :],
                in_=bmT_d[4 * cc + i:4 * cc + i + 1, :].broadcast_to([32, 128])),
              reads=[tok_bmd], writes=[sl_bmE], inc=16)

    sl_wgq_sh, sl_wuq_sh, sl_wdq_sh = slot(wgq_sh), slot(wuq_sh), slot(wdqT_sh)
    cc_toks = {}
    for (src_h, qdst, dslot, key) in ((wg_sh, wgq_sh, sl_wgq_sh, "wg"),
                                      (wu_sh, wuq_sh, sl_wuq_sh, "wu")):
        for t in range(SH_WG // 128):
            row0 = t * 128
            src, ssl = (arA, sl_arA) if t % 2 == 0 else (arB, sl_arB)
            E("sp", lambda s, src=src, src_h=src_h, row0=row0: s.dma_start(
                out=src[:], in_=src_h[row0:row0 + 128, :]),
              writes=[ssl], inc=16)
            emit_qdq(lambda o, n, src=src: src[:, o:o + n],
                     lambda o, n: wq_tok[:, o:o + n],
                     D, False, None, upass_views(src, wq_tok), ssl, sl_wq)
            E("po", lambda g, qdst=qdst, row0=row0: g.dma_start(
                out=qdst.ap()[row0:row0 + 128, :], in_=wq_tok[:]),
              reads=[sl_wq], writes=[dslot], inc=16)
        gdst = wgq_d if key == "wg" else wuq_d
        gsl = slot(gdst)
        cc_toks[key] = E("po", lambda g, qdst=qdst, gdst=gdst: g.collective_compute(
            "AllGather", ALU.bypass, replica_groups=[list(range(NCORES))],
            ins=[qdst.ap().opt()], outs=[gdst.ap().opt()]),
          reads=[dslot], writes=[gsl], inc=1)

    mega_off = 0
    for mi, M in enumerate(MEGAS):
        g256d = (mi == len(MEGAS) - 1)
        KMc = M // 128
        for rt in range(4):
            row0 = rt * 128
            src, ssl = (arA, sl_arA) if rt % 2 == 0 else (arB, sl_arB)
            E("sp", lambda s, src=src, row0=row0, mo=mega_off, M=M: s.dma_start(
                out=src[:, 0:M], in_=wd_sh[row0:row0 + 128, mo:mo + M]),
              writes=[ssl], inc=16)
            pv = None
            if g256d:
                base = 10752 - mega_off
                sv = src[:, base:base + 256].rearrange(
                    "p (k b) -> p k b", b=16)[:, :, 8:16]
                dv = wq_tok[:, base:base + 256].rearrange(
                    "p (k b) -> p k b", b=16)[:, :, 8:16]
                pv = (sv, dv)
            scol = scale_sb[:, rt:rt + 1]
            emit_qdq(lambda o, n, src=src: src[:, o:o + n],
                     lambda o, n: wq_tok[:, o:o + n],
                     M, g256d, scol, pv, ssl, sl_wq, extra_reads=[sl_scale])
            E("sp", lambda s, KMc=KMc, M=M: s.dma_start(
                out=trst[:, 0:KMc, :], in_=wq_tok[:, 0:M], transpose=True),
              reads=[sl_wq], writes=[sl_slab], inc=16)
            E("po", lambda g, mo=mega_off, M=M, row0=row0, KMc=KMc: g.dma_start(
                out=wdqT_sh.ap()[mo:mo + M, row0:row0 + 128]
                    .rearrange("(k p) q -> p k q", p=128),
                in_=trst[:, 0:KMc, :]),
              reads=[sl_slab], writes=[sl_wdq_sh], inc=16)
        mega_off += M
    gsl_wd = slot(wdT_g)
    cc_toks["wd"] = E("po", lambda g: g.collective_compute(
        "AllGather", ALU.bypass, replica_groups=[list(range(NCORES))],
        ins=[wdqT_sh.ap().opt()], outs=[wdT_g.ap().opt()]),
      reads=[sl_wdq_sh], writes=[gsl_wd], inc=1)

    uview = u_sb[:].rearrange("p (tt f) -> p tt f", f=128)
    E("ve", lambda v: v.memset(u_sb[:], 0.0), writes=[sl_usb])

    for tt in range(NTT):
        r0 = tt * 128
        E("sp", lambda s, r0=r0: s.dma_start(out=arA[:], in_=x_tok[r0:r0 + 128, :]),
          writes=[sl_arA], inc=16)
        E("ve", lambda v: v.reduce_sum(out=st["s1"][:], in_=arA[:], axis=AXIS.X),
          reads=[sl_arA], writes=[slot(st["s1"])])
        E("ac", lambda s: s.activation(out=arXN[:], in_=arA[:], func=ACT.Square,
                                       accum_out=st["s2"][:]),
          reads=[sl_arA], writes=[sl_arXN, slot(st["s2"])])
        E("ve", lambda v: v.tensor_scalar_mul(st["mean"][:], st["s1"][:], 1.0 / D),
          reads=[slot(st["s1"])], writes=[slot(st["mean"])])
        E("ve", lambda v: v.tensor_scalar_mul(st["ex2"][:], st["s2"][:], 1.0 / D),
          reads=[slot(st["s2"])], writes=[slot(st["ex2"])])
        E("ve", lambda v: v.tensor_tensor(out=st["m2"][:], in0=st["mean"][:],
                                          in1=st["mean"][:], op=ALU.mult),
          reads=[slot(st["mean"])], writes=[slot(st["m2"])])
        E("ve", lambda v: v.tensor_tensor(out=st["var"][:], in0=st["ex2"][:],
                                          in1=st["m2"][:], op=ALU.subtract),
          reads=[slot(st["ex2"]), slot(st["m2"])], writes=[slot(st["var"])])
        E("ve", lambda v: v.tensor_scalar_add(st["var"][:], st["var"][:], EPS),
          reads=[slot(st["var"])], writes=[slot(st["var"])])
        E("ac", lambda s: s.activation(out=st["sd"][:], in_=st["var"][:], func=ACT.Sqrt),
          reads=[slot(st["var"])], writes=[slot(st["sd"])])
        E("ve", lambda v: v.reciprocal(out=st["rstd"][:], in_=st["sd"][:]),
          reads=[slot(st["sd"])], writes=[slot(st["rstd"])])
        E("ve", lambda v: v.tensor_scalar(out=arXN[:], in0=arA[:], scalar1=st["mean"][:],
                                          scalar2=st["rstd"][:], op0=ALU.subtract,
                                          op1=ALU.mult),
          reads=[sl_arA, slot(st["mean"]), slot(st["rstd"])], writes=[sl_arXN])
        E("ve", lambda v: v.tensor_copy(out=wq_tok[:], in_=arXN[:]),
          reads=[sl_arXN], writes=[sl_wq])
        E("sp", lambda s: s.dma_start(out=trst[:, 0:C, :], in_=wq_tok[:], transpose=True),
          reads=[sl_wq], writes=[sl_slab], inc=16)
        E("po", lambda g, r0=r0: g.dma_start(
            out=xnfm_d[:, :, r0:r0 + 128].transpose([1, 0, 2]), in_=trst[:, 0:C, :]),
          reads=[sl_slab], inc=16)
        lrA_v = lrA_sb[:].rearrange("p (c f) -> p c f", f=64)
        for cc in range(C):
            E("pe", lambda t, cc=cc: t.matmul(
                psb[0][0:128, 0:64], trst[:, cc, :], lrA_v[:, cc, :],
                start=(cc == 0), stop=(cc == C - 1)),
              reads=[sl_slab, sl_lrA],
              writes=([sl_ps[0]] if cc in (0, C - 1) else []))
        E("ve", lambda v, tt=tt: v.tensor_copy(out=uview[:, tt, 0:64],
                                               in_=psb[0][0:128, 0:64]),
          reads=[sl_ps[0]], writes=[sl_usb])
        E("sp", lambda s, r0=r0: s.dma_start(out=arB[:], in_=x_tokU[r0:r0 + 128, :]),
          writes=[sl_arB], inc=16)
        E("ve", lambda v: v.tensor_scalar(out=arB[:], in0=arB[:], scalar1=st["mean"][:],
                                          scalar2=st["rstd"][:], op0=ALU.subtract,
                                          op1=ALU.mult),
          reads=[sl_arB, slot(st["mean"]), slot(st["rstd"])], writes=[sl_arB])
        emit_qdq(lambda o, n: arB[:, o:o + n], lambda o, n: wq_tok[:, o:o + n],
                 D, False, None, upass_views(arB, wq_tok), sl_arB, sl_wq)
        E("sp", lambda s: s.dma_start(out=trst[:, 0:C, :], in_=wq_tok[:], transpose=True),
          reads=[sl_wq], writes=[sl_slab], inc=16)
        E("po", lambda g, r0=r0: g.dma_start(
            out=xqfm_d[:, :, r0:r0 + 128].transpose([1, 0, 2]), in_=trst[:, 0:C, :]),
          reads=[sl_slab], inc=16)

    E("sp", lambda s: s.dma_start(out=uT[:], in_=u_sb[:], transpose=True),
      reads=[sl_usb], writes=[sl_uT], inc=16)

    for nf in range(8):
        for cc in range(C):
            ib = cc % 2
            E("sp", lambda s, nf=nf, cc=cc, ib=ib: s.dma_start(
                out=wa_raw[ib][:],
                in_=watt[cc * 128:(cc + 1) * 128, nf * 512:(nf + 1) * 512]),
              writes=[sl_wa[ib]], inc=16)
            E("ve", lambda v, nf=nf, ib=ib, cc=cc: v.tensor_tensor(
                out=wm[ib][:].rearrange("p (nb w) -> p nb w", w=32),
                in0=wa_raw[ib][:].rearrange("p (nb w) -> p nb w", w=32),
                in1=bmT_exp[:, cc, nf * 16:(nf + 1) * 16].unsqueeze(2)
                    .broadcast_to([128, 16, 32]),
                op=ALU.mult),
              reads=[sl_wa[ib], sl_bmE], writes=[sl_wm[ib]])
            E("ac", lambda s, cc=cc, ib=ib: s.dma_start(
                out=xs_st[ib][:], in_=xnfm_d[cc, :, :]),
              writes=[sl_xs[ib]], inc=16)
            for tt in range(NTT):
                E("pe", lambda t, cc=cc, tt=tt, ib=ib: t.matmul(
                    psb[tt][:, :], xs_st[ib][:, tt * 128:(tt + 1) * 128], wm[ib][:],
                    start=(cc == 0), stop=False),
                  reads=[sl_xs[ib], sl_wm[ib]],
                  writes=([sl_ps[tt]] if cc == 0 else []))
        for tt in range(NTT):
            E("pe", lambda t, nf=nf, tt=tt: t.matmul(
                psb[tt][:, :], ones_bf, bias_bf[:, nf * 512:(nf + 1) * 512],
                start=False, stop=True),
              reads=[sl_ones, sl_bias], writes=[sl_ps[tt]])
        for tt in range(NTT):
            ib = tt % 2
            E("ve", lambda v, tt=tt, ib=ib: v.tensor_copy(out=ev_st[ib][:],
                                                          in_=psb[tt][:, :]),
              reads=[sl_ps[tt]], writes=[sl_ev[ib]])
            E("po", lambda g, nf=nf, tt=tt, ib=ib: g.dma_start(
                out=s_d[tt * 128:(tt + 1) * 128, nf * 512:(nf + 1) * 512],
                in_=ev_st[ib][:]),
              reads=[sl_ev[ib]], inc=16)

    pending_hq = []

    def flush_hq():
        for fn in pending_hq:
            fn()
        del pending_hq[:]

    mega_off = 0
    for mi, M in enumerate(MEGAS):
        KM = M // 128
        hf_tiles = []
        o = 0
        while o < M:
            ts = min(512, M - o)
            hf_tiles.append((o, ts))
            o += ts
        for (ho, hts) in hf_tiles:
            g256 = (mi == len(MEGAS) - 1 and mega_off + ho + hts > H - 256)
            szv = (sz1 if hts == 512
                   else arXN[:].rearrange("p (t f) -> p t f", f=256))
            for zi, (wsrc, cck) in ((0, (wgq_d, "wg")), (1, (wuq_d, "wu"))):
                for rb in range(hts // 128):
                    row0 = mega_off + ho + rb * 128
                    if rb % 2 == 0:
                        buf, bsl = wq_tok[:], sl_wq
                    else:
                        buf, bsl = arB_bf, sl_arB
                    E("ac", lambda s, wsrc=wsrc, row0=row0, buf=buf, ck=cck: s.dma_start(
                        out=buf, in_=wsrc.ap()[row0:row0 + 128, :]),
                      reads=[cc_toks[cck]], writes=[bsl], inc=16)
                    E("sp", lambda s, rb=rb, buf=buf: s.dma_start(
                        out=wgq_view[:, :, rb, :], in_=buf, transpose=True),
                      reads=[bsl], writes=[sl_slab], inc=16)
                if zi == 0:
                    flush_hq()
                for cc in range(C):
                    ib = cc % 2
                    E("ac", lambda s, cc=cc, ib=ib: s.dma_start(
                        out=xs_st[ib][:], in_=xqfm_d[cc, :, :]),
                      writes=[sl_xs[ib]], inc=16)
                    for tt in range(NTT):
                        E("pe", lambda t, cc=cc, ib=ib, tt=tt, hts=hts: t.matmul(
                            psb[tt][:, 0:hts],
                            xs_st[ib][:, tt * 128:(tt + 1) * 128],
                            slab[:].rearrange("p (c f) -> p c f", f=512)[:, cc, 0:hts],
                            start=(cc == 0), stop=(cc == C - 1)),
                          reads=[sl_xs[ib], sl_slab],
                          writes=([sl_ps[tt]] if cc in (0, C - 1) else []))
                if zi == 0:
                    for tt in range(NTT):
                        E("ac", lambda s, tt=tt, hts=hts, szv=szv: s.activation(
                            out=szv[:, tt, 0:hts], in_=psb[tt][:, 0:hts], func=ACT.Silu),
                          reads=[sl_ps[tt]], writes=[sl_arXN])
            for tt in range(NTT):
                E("ve", lambda v, tt=tt, hts=hts, szv=szv: v.tensor_tensor(
                    out=szv[:, tt, 0:hts], in0=szv[:, tt, 0:hts],
                    in1=psb[tt][:, 0:hts], op=ALU.mult),
                  reads=[sl_arXN, sl_ps[tt]], writes=[sl_arXN])
            flatF = NTT * hts
            pv = None
            if g256:
                sv = arXN[:, 0:flatF].rearrange(
                    "p (g k b) -> p g k b", k=16, b=16)[:, :, :, 8:16]
                dv = hq8[:, 0:flatF].rearrange(
                    "p (g k b) -> p g k b", k=16, b=16)[:, :, :, 8:16]
                pv = (sv, dv)
            emit_qdq(lambda o_, n_: arXN[:, o_:o_ + n_],
                     lambda o_, n_: hq8[:, o_:o_ + n_],
                     flatF, False, None, pv, sl_arXN, sl_hq8,
                     all256=(hts == 256))
            for tt in range(NTT):
                pending_hq.append(
                    lambda ho=ho, tt=tt, hts=hts: E(
                        "sp", lambda s, ho=ho, tt=tt, hts=hts: s.dma_start(
                            out=hq_meg[:, ho // 128:(ho + hts) // 128,
                                       tt * 128:(tt + 1) * 128],
                            in_=hq8[:, tt * hts:(tt + 1) * hts], transpose=True),
                        reads=[sl_hq8], writes=[sl_hq], inc=16))
        flush_hq()
        for df in range(8):
            for k in range(KM):
                E("sp", lambda s, df=df, k=k, mo=mega_off: s.dma_start(
                    out=wdq_dir[:, k, :],
                    in_=wdT_g.ap()[df, mo + k * 128:mo + (k + 1) * 128, :]),
                  reads=[cc_toks["wd"]], writes=[sl_slab], inc=16)
            for tt in range(NTT):
                for k in range(KM):
                    last = (k == KM - 1 and mi != 0)
                    E("pe", lambda t, k=k, tt=tt: t.matmul(
                        psb[tt][:, :], hq_meg[:, k, tt * 128:(tt + 1) * 128],
                        wdq_dir[:, k, :],
                        start=(k == 0), stop=(k == KM - 1 and mi != 0)),
                      reads=[sl_hq, sl_slab],
                      writes=([sl_ps[tt]] if (k == 0 or last) else []))
                if mi == 0:
                    E("pe", lambda t, tt=tt, df=df: t.matmul(
                        psb[tt][:, :], uT[0:64, tt, :],
                        lrB_sb[:, df * 512:(df + 1) * 512],
                        start=False, stop=True),
                      reads=[sl_uT, sl_lrB], writes=[sl_ps[tt]])
                ib = tt % 2
                E("ve", lambda v, tt=tt, ib=ib: v.tensor_copy(
                    out=dstage[ib][:], in_=psb[tt][:, :]),
                  reads=[sl_ps[tt]], writes=[sl_dst[ib]])
                E("po", lambda g, tt=tt, df=df, ib=ib, mi=mi: g.dma_start(
                    out=acc_d[tt * 128:(tt + 1) * 128, df * 512:(df + 1) * 512],
                    in_=dstage[ib][:],
                    accum_op=(ALU.bypass if mi == 0 else ALU.add)),
                  reads=[sl_dst[ib]], inc=16)
        mega_off += M

    for tt in range(NTT):
        r0 = tt * 128
        E("sp", lambda s, r0=r0: s.dma_start(out=arA[:], in_=x_tok[r0:r0 + 128, :]),
          writes=[sl_arA], inc=16)
        E("po", lambda g, r0=r0: g.dma_start(out=arB[:], in_=acc_d[r0:r0 + 128, :]),
          writes=[sl_arB], inc=16)
        E("sp", lambda s, r0=r0: s.dma_start(out=wq_tok[:], in_=s_d[r0:r0 + 128, :]),
          writes=[sl_wq], inc=16)
        E("ac", lambda s: s.activation(out=arXN[:], in_=arB[:], func=ACT.Silu),
          reads=[sl_arB], writes=[sl_arXN])
        E("ve", lambda v: v.tensor_tensor(out=arXN[:], in0=arXN[:], in1=wq_tok[:],
                                          op=ALU.mult),
          reads=[sl_arXN, sl_wq], writes=[sl_arXN])
        E("ve", lambda v: v.tensor_tensor(out=arB[:], in0=arXN[:], in1=arA[:],
                                          op=ALU.add),
          reads=[sl_arXN, sl_arA], writes=[sl_arB])
        E("po", lambda g, r0=r0: g.dma_start(out=out_d[r0:r0 + 128, :], in_=arB[:]),
          reads=[sl_arB], inc=16)

    with nc.Block() as block:
        def runner(engname, handle):
            for (waits, fn, inc) in p.steps[engname]:
                for (peng, pcnt) in waits:
                    handle.wait_ge(sems[peng], pcnt)
                fn(handle).then_inc(sems[engname], inc)
            if engname == "po":
                handle.wait_ge(sems["po"], p.counts["po"])

        @block.sync
        def _(h):
            runner("sp", h)

        @block.vector
        def _(h):
            runner("ve", h)

        @block.scalar
        def _(h):
            runner("ac", h)

        @block.tensor
        def _(h):
            runner("pe", h)

        @block.gpsimd
        def _(h):
            runner("po", h)

    for t in reversed(ctx):
        t.__exit__(None, None, None)
    return nc


_NC_CACHE = {}


def kernel(L0_attn_data, L0_attn_mask, L0_attn_weight, L0_attn_bias,
           L0_ffn_w_gate, L0_ffn_w_up, L0_ffn_w_down, L0_ffn_mlp_output_scale,
           L0_ffn_lr_A, L0_ffn_lr_B, L0_ffn_up_reorder_index,
           L0_ffn_down_reorder_index):
    x = np.ascontiguousarray(np.asarray(L0_attn_data, dtype=np.float32).reshape(8192, D))
    up_perm = np.asarray(L0_ffn_up_reorder_index, dtype=np.int64)
    down_perm = np.asarray(L0_ffn_down_reorder_index, dtype=np.int64)
    pU = up_perm[ilv_index(D)]
    hsel = down_perm[ilv_index(H)]

    watt_T = np.ascontiguousarray(np.asarray(L0_attn_weight, dtype=np.float32).T)
    mask_r = np.ascontiguousarray(
        np.asarray(L0_attn_mask, dtype=np.int32)
        .reshape(128, 32, 128, 32).transpose(0, 2, 1, 3).reshape(128, 128, 1024))
    bias_in = np.ascontiguousarray(np.asarray(L0_attn_bias, dtype=np.float32)[None, :])
    wg_hh = np.asarray(L0_ffn_w_gate, dtype=np.float32)[hsel][:, pU]
    wu_hh = np.asarray(L0_ffn_w_up, dtype=np.float32)[hsel][:, pU]
    wgp = np.vstack([wg_hh, np.zeros((HP - H, D), dtype=np.float32)])
    wup = np.vstack([wu_hh, np.zeros((HP - H, D), dtype=np.float32)])
    wd_hh = np.asarray(L0_ffn_w_down, dtype=np.float32)[:, hsel]
    scale_f = np.asarray(L0_ffn_mlp_output_scale, dtype=np.float32)
    lrA_in = np.ascontiguousarray(
        np.asarray(L0_ffn_lr_A, dtype=np.float32).reshape(C, 128, 64)
        .transpose(1, 0, 2).reshape(128, C * 64))
    lrB_in = np.ascontiguousarray(np.asarray(L0_ffn_lr_B, dtype=np.float32))

    if "main" not in _NC_CACHE:
        _NC_CACHE["main"] = build_nc()
    nc = _NC_CACHE["main"]

    in_maps = []
    for ci in range(NCORES):
        xs = np.ascontiguousarray(x[ci * T:(ci + 1) * T, :])
        xsU = np.ascontiguousarray(xs[:, pU])
        in_maps.append({
            "x_tok": xs, "x_tokU": xsU, "watt": watt_T, "mask_r": mask_r,
            "bias_in": bias_in,
            "wg_sh": np.ascontiguousarray(wgp[ci * SH_WG:(ci + 1) * SH_WG]),
            "wu_sh": np.ascontiguousarray(wup[ci * SH_WG:(ci + 1) * SH_WG]),
            "wd_sh": np.ascontiguousarray(wd_hh[ci * SH_WD:(ci + 1) * SH_WD]),
            "scale_sh": np.ascontiguousarray(
                scale_f[ci * SH_WD:(ci + 1) * SH_WD].reshape(4, 128).T),
            "lrA_in": lrA_in, "lrB_in": lrB_in,
        })
    res = run_bass_kernel_spmd(nc, in_maps, core_ids=list(range(NCORES)))
    out = np.concatenate([np.asarray(r["out"]) for r in res.results], axis=0)
    return out.reshape(2, 4096, D)
